# version 02
# baseline (speedup 1.0000x reference)
from contextlib import ExitStack

import numpy as np

import concourse.bass as bass
import concourse.mybir as mybir
from concourse.bass_utils import run_bass_kernel_spmd

F32 = mybir.dt.float32
BF16 = mybir.dt.bfloat16
FP16 = mybir.dt.float16
AX = mybir.AxisListType.X
OP = mybir.AluOpType
AF = mybir.ActivationFunctionType

B, S, DM = 4, 2048, 2048
T = 1024
H, HD = 16, 128
KVC, QC, RD = 512, 1536, 64
IF = 4096
DC = DM // 128
OCH = IF // 128
QCC = QC // 128
KVCC = KVC // 128
SKC = S // 128
ISD = 1.0 / float(np.sqrt(HD))
RB = 1536.0
NUP, NDN = OCH, DC


def build_nc(upto=3):
    nc = bass.Bass("TRN2", target_bir_lowering=False, debug=False,
                   detect_race_conditions=False)

    di = lambda n, s: nc.dram_tensor(n, s, F32, kind="ExternalInput").ap()
    x_in = di("x_fm", [DM, T])
    wdkv, wuk, wuv = di("wdkv", [DM, KVC]), di("wuk", [KVC, DM]), di("wuv", [KVC, DM])
    wdq, wuq = di("wdq", [DM, QC]), di("wuq", [QC, DM])
    wqr, wkr = di("wqr", [HD, RD]), di("wkr", [HD, RD])
    wo = di("wo", [DM, DM])
    wupT, wdownT = di("wupT", [DM, IF]), di("wdownT", [IF, DM])
    b_dkv, b_uk = di("b_dkv", [128, KVCC]), di("b_uk", [128, DC])
    b_dq, b_uq, b_o = di("b_dq", [128, QCC]), di("b_uq", [128, DC]), di("b_o", [128, DC])
    buv_b = di("buv_b", [128, DM])
    cosq, sinq = di("cosq", [RD // 2, T]), di("sinq", [RD // 2, T])
    cosk, sink = di("cosk", [RD // 2, S]), di("sink", [RD // 2, S])
    ident_in = di("ident_in", [128, 128])
    out_fm = nc.dram_tensor("out_fm", [DM, T], F32, kind="ExternalOutput").ap()
    tern_up_d = nc.dram_tensor("tern_up_d", [NUP, 128, DC * 128], BF16).ap()
    tern_dn_d = nc.dram_tensor("tern_dn_d", [NDN, 128, OCH * 128], BF16).ap()
    xres_d = nc.dram_tensor("xres_d", [128, DC * T], F32).ap()
    cc_in = nc.dram_tensor("cc_in", [KVCC * 128 * T], BF16).ap()
    cc_out = nc.dram_tensor("cc_out", [2, KVCC * 128 * T], BF16).ap()

    es = ExitStack()
    sb = lambda n, s, d=F32: es.enter_context(nc.sbuf_tensor(n, s, d))

    bc_r = sb("bc_r", [128, T])
    bc_b = sb("bc_b", [128, T])
    bc_f = sb("bc_f", [128, T])
    u32 = sb("u32", [128, T])
    uh16 = sb("uh16", [128, T], FP16)
    sq16 = [sb(f"sq16_{i}", [128, T], BF16) for i in range(2)]
    ident = sb("ident", [128, 128])
    ones_bf = sb("ones_bf", [128, 128], BF16)
    ones_f = sb("ones_f", [128, 128])
    st_tm = sb("st_tm", [128, 32])
    st_row = sb("st_row", [1, T])
    srow16 = sb("srow16", [1, T], BF16)
    sc = sb("sc", [128, 8])
    padt = sb("padt", [128, 8])
    epsln = sb("epsln", [128, 1])
    accw = sb("accw", [128, 64])
    tiny = sb("tiny", [1, 16])
    bdkv_sb = sb("bdkv_sb", [128, KVCC])
    buk_sb = sb("buk_sb", [128, DC])
    bdq_sb = sb("bdq_sb", [128, QCC])
    buq_sb = sb("buq_sb", [128, DC])
    bo_sb = sb("bo_sb", [128, DC])

    PS_A = es.enter_context(nc.psum_tensor("PS_A", [128, T], F32))
    PS_B = es.enter_context(nc.psum_tensor("PS_B", [128, T], F32))
    PS_C = es.enter_context(nc.psum_tensor("PS_C", [128, T], F32))
    PS_D = es.enter_context(nc.psum_tensor("PS_D", [128, T], F32))

    sems = {}
    vals = {}
    for n in ("dCN", "dWU", "dWD", "dTU", "dTD", "dXC", "dXO", "dAW", "dCC",
              "dRW",
              "aW", "sqP", "qUP", "mUP", "eUP", "hUP", "qDN", "mDN", "eDN",
              "mST", "eEX", "mCX", "mAT", "eAT", "vAT", "cCC",
              "gT", "gV", "gS", "gG"):
        sems[n] = es.enter_context(nc.semaphore(n))
        vals[n] = 0
    SEM = lambda n: sems[n]

    def ti(inst, n, k=1):
        vals[n] += k
        inst.then_inc(sems[n], k)
        return vals[n]

    cur = lambda n: vals[n]

    managed = {}
    _seq = [0]

    def alloc(name, shape, dtype=F32, side="left"):
        cm = nc.sbuf_tensor(name, shape, dtype, side=side)
        h = cm.__enter__()
        _seq[0] += 1
        managed[name] = (_seq[0], cm)
        return h

    def free(*names):
        for n in sorted(names, key=lambda n: -managed[n][0]):
            managed.pop(n)[1].__exit__(None, None, None)

    with es:
        with nc.Block() as block:
            def sync_all(*extra):
                snap = [(n, vals[n]) for n in sems if vals[n] > 0]
                for eng in ("tensor", "vector", "scalar", "gpsimd", "sync"):
                    def _w(e):
                        for n, v in snap:
                            e.wait_ge(sems[n], v)
                    getattr(block, eng)(_w)

            def vpad(v, n=1):
                for _ in range(n):
                    v.tensor_scalar(out=padt[:], in0=padt[:], scalar1=1.0,
                                    scalar2=None, op0=OP.mult)

            def mark(eng, name):
                def _f(e):
                    e.sem_inc(sems[name], 1)
                getattr(block, eng)(_f)
                vals[name] += 1
                return vals[name]

            def bcast_tm(cols8, out_tile):
                v0 = cur("gV")

                @block.tensor
                def _(t):
                    t.wait_ge(SEM("gV"), v0)
                    for j, col in enumerate(cols8):
                        i = t.transpose(PS_D[0:1, j * 128:(j + 1) * 128],
                                        st_tm[:, col:col + 1], ident[:])
                    ti(i, "gT")
                tp = cur("gT")

                @block.vector
                def _(v):
                    v.wait_ge(SEM("gT"), tp)
                    ti(v.tensor_copy(out=st_row[0:1, 0:T], in_=PS_D[0:1, 0:T]),
                       "gV")
                rw = cur("gV")

                @block.tensor
                def _(t):
                    t.wait_ge(SEM("gV"), rw)
                    t.matmul(PS_C[:, 0:512], ones_f[0:1, 0:128],
                             st_row[0:1, 0:512], start=True, stop=True)
                    ti(t.matmul(PS_C[:, 512:1024], ones_f[0:1, 0:128],
                                st_row[0:1, 512:1024], start=True, stop=True),
                       "gT")
                bm = cur("gT")

                @block.vector
                def _(v):
                    v.wait_ge(SEM("gT"), bm)
                    ti(v.tensor_copy(out=out_tile[:], in_=PS_C[:, 0:T]), "gV")

            act16 = alloc("act16", [128, DC * T], BF16, side="right")
            x_sb = alloc("x_sb", [128, DC * T], side="right")

            @block.sync
            def _(sy):
                for dst, src in [
                    (x_sb[:].rearrange("p (c t) -> p c t", c=DC),
                     x_in.rearrange("(c p) t -> p c t", p=128)),
                    (ident[:], ident_in),
                    (bdkv_sb[:], b_dkv),
                    (buk_sb[:], b_uk),
                    (bdq_sb[:], b_dq),
                    (buq_sb[:], b_uq),
                    (bo_sb[:], b_o),
                ]:
                    ti(sy.dma_start(out=dst, in_=src), "dCN", 16)
            consts_dma = cur("dCN")

            @block.vector
            def _(v):
                v.memset(ones_bf[:], 1.0)
                v.memset(ones_f[:], 1.0)
                v.memset(accw[:], 0.0)
                v.memset(st_tm[:], 0.0)
                v.memset(st_row[:], 0.0)
                v.memset(padt[:], 0.0)
                v.memset(epsln[:], 1e-5)
                v.wait_ge(SEM("dCN"), consts_dma)
                ti(v.tensor_scalar(out=buq_sb[:], in0=buq_sb[:], scalar1=ISD,
                                   scalar2=None, op0=OP.mult), "gV")
            consts_v = cur("gV")

            wstg = [alloc(f"wstg{i}", [128, 4096]) for i in range(2)]
            babs = alloc("babs", [128, 4096], BF16)

            @block.gpsimd
            def _(g):
                for i in range(NUP):
                    if i >= 2:
                        g.wait_ge(SEM("aW"), i - 1)
                    ti(g.dma_start(
                        out=wstg[i % 2][:, 0:2048].rearrange(
                            "p (c o) -> p c o", o=128),
                        in_=wupT[:, i * 128:(i + 1) * 128].rearrange(
                            "(c p) o -> p c o", p=128)), "dWU", 16)
                for i in range(NDN):
                    g.wait_ge(SEM("aW"), NUP + i - 1 if i >= 2 else NUP - 1 + i)
                    ti(g.dma_start(
                        out=wstg[i % 2][:, 0:4096].rearrange(
                            "p (c o) -> p c o", o=128),
                        in_=wdownT[:, i * 128:(i + 1) * 128].rearrange(
                            "(c p) o -> p c o", p=128)), "dWU", 16)

            @block.scalar
            def _(s_):
                for i in range(NUP):
                    s_.wait_ge(SEM("dWU"), (i + 1) * 16)
                    ti(s_.activation(out=babs[:, 0:2048],
                                     in_=wstg[i % 2][:, 0:2048], func=AF.Abs,
                                     accum_out=accw[:, i:i + 1]), "aW")
                for i in range(NDN):
                    s_.wait_ge(SEM("dWU"), (NUP + i + 1) * 16)
                    s_.activation(out=babs[:, 0:2048],
                                  in_=wstg[i % 2][:, 0:2048], func=AF.Abs,
                                  accum_out=accw[:, 32 + 2 * i:33 + 2 * i])
                    ti(s_.activation(out=babs[:, 0:2048],
                                     in_=wstg[i % 2][:, 2048:4096], func=AF.Abs,
                                     accum_out=accw[:, 33 + 2 * i:34 + 2 * i]),
                       "aW")
            abs_done = cur("aW")

            @block.vector
            def _(v):
                v.wait_ge(SEM("aW"), abs_done)
                v.reduce_sum(out=st_tm[:, 16:17], in_=accw[:, 0:32], axis=AX)
                ti(v.reduce_sum(out=st_tm[:, 17:18], in_=accw[:, 32:64],
                                axis=AX), "gV")
            v1 = cur("gV")

            @block.tensor
            def _(t):
                t.wait_ge(SEM("gV"), v1)
                ti(t.matmul(PS_C[0:1, 0:2], ones_f[:, 0:1], st_tm[:, 16:18],
                            start=True, stop=True), "gT")
            t1 = cur("gT")

            @block.vector
            def _(v):
                v.wait_ge(SEM("gT"), t1)
                v.tensor_scalar(out=tiny[0:1, 0:2], in0=PS_C[0:1, 0:2],
                                scalar1=1.0 / (DM * IF), scalar2=1e-5,
                                op0=OP.mult, op1=OP.max)
                vpad(v)
                ti(v.tensor_copy(out=st_row[0:1, 0:2], in_=tiny[0:1, 0:2]),
                   "gV")
            v2 = cur("gV")

            @block.tensor
            def _(t):
                t.wait_ge(SEM("gV"), v2)
                ti(t.matmul(PS_C[:, 0:2], ones_f[0:1, 0:128],
                            st_row[0:1, 0:2], start=True, stop=True), "gT")
            t2 = cur("gT")

            @block.vector
            def _(v):
                v.wait_ge(SEM("gT"), t2)
                v.tensor_copy(out=sc[:, 4:6], in_=PS_C[:, 0:2])
                vpad(v)
                v.tensor_scalar(out=sc[:, 0:1], in0=sc[:, 4:5], scalar1=0.5,
                                scalar2=None, op0=OP.mult)
                v.tensor_scalar(out=sc[:, 1:2], in0=sc[:, 4:5], scalar1=-0.5,
                                scalar2=None, op0=OP.mult)
                v.tensor_scalar(out=sc[:, 2:3], in0=sc[:, 5:6], scalar1=0.5,
                                scalar2=None, op0=OP.mult)
                ti(v.tensor_scalar(out=sc[:, 3:4], in0=sc[:, 5:6],
                                   scalar1=-0.5, scalar2=None, op0=OP.mult),
                   "gV")
            sc_v = cur("gV")

            def ln_phase(lnx, quant, mpos, mneg):
                sq0 = cur("sqP")
                st0 = cur("gT")

                @block.scalar
                def _(s_):
                    s_.wait_ge(SEM("gV"), cur("gV"))
                    for c in range(DC):
                        if c >= 2:
                            s_.wait_ge(SEM("gT"), st0 + c - 1)
                        ti(s_.activation(out=sq16[c % 2][:],
                                         in_=x_sb[:, c * T:(c + 1) * T],
                                         func=AF.Square), "sqP")

                @block.tensor
                def _(t, gv0=None):
                    t.wait_ge(SEM("gV"), cur("gV"))
                    t.wait_ge(SEM("gS"), cur("gS"))
                    t.wait_ge(SEM("eAT"), cur("eAT"))
                    t.wait_ge(SEM("vAT"), cur("vAT"))
                    for c in range(DC):
                        t.matmul(PS_C[:, 0:512], ones_f[:, 0:128],
                                 x_sb[:, c * T:c * T + 512],
                                 start=(c == 0), stop=(c == DC - 1))
                        t.matmul(PS_C[:, 512:1024], ones_f[:, 0:128],
                                 x_sb[:, c * T + 512:(c + 1) * T],
                                 start=(c == 0), stop=(c == DC - 1))
                        t.wait_ge(SEM("sqP"), sq0 + c + 1)
                        t.matmul(PS_D[:, 0:512], ones_bf[:, 0:128],
                                 sq16[c % 2][:, 0:512],
                                 start=(c == 0), stop=(c == DC - 1))
                        ti(t.matmul(PS_D[:, 512:1024], ones_bf[:, 0:128],
                                    sq16[c % 2][:, 512:1024],
                                    start=(c == 0), stop=(c == DC - 1)), "gT")
                stats_t = cur("gT")

                @block.vector
                def _(v):
                    v.wait_ge(SEM("gT"), stats_t)
                    v.tensor_scalar(out=bc_b[:], in0=PS_C[:, 0:T],
                                    scalar1=1.0 / DM, scalar2=None,
                                    op0=OP.mult)
                    v.tensor_tensor(out=u32[:], in0=bc_b[:], in1=bc_b[:],
                                    op=OP.mult)
                    ti(v.scalar_tensor_tensor(out=u32[:], in0=PS_D[:, 0:T],
                                              scalar=1.0 / DM, in1=u32[:],
                                              op0=OP.mult, op1=OP.subtract),
                       "gV")
                var_v = cur("gV")

                @block.scalar
                def _(s_):
                    s_.wait_ge(SEM("gV"), var_v)
                    s_.activation(out=bc_f[:], in_=u32[:], func=AF.Ln,
                                  bias=epsln[:, 0:1])
                    ti(s_.activation(out=bc_r[:], in_=bc_f[:], func=AF.Exp,
                                     scale=-0.5), "gS")
                r_s = cur("gS")

                @block.vector
                def _(v):
                    v.wait_ge(SEM("gS"), r_s)
                    v.tensor_tensor(out=bc_b[:], in0=bc_b[:], in1=bc_r[:],
                                    op=OP.mult)
                    for c in range(DC):
                        v.tensor_tensor(out=u32[:],
                                        in0=x_sb[:, c * T:(c + 1) * T],
                                        in1=bc_r[:], op=OP.mult)
                        if not quant:
                            i = v.tensor_tensor(out=lnx[:, c * T:(c + 1) * T],
                                                in0=u32[:], in1=bc_b[:],
                                                op=OP.subtract)
                        else:
                            v.tensor_tensor(out=u32[:], in0=u32[:],
                                            in1=bc_b[:], op=OP.subtract)
                            if c == 0:
                                v.tensor_copy(out=mpos[:, 0:T], in_=u32[:])
                                i = v.tensor_copy(out=mneg[:, 0:T], in_=u32[:])
                            else:
                                v.tensor_tensor(out=mpos[:, 0:T],
                                                in0=mpos[:, 0:T],
                                                in1=u32[:], op=OP.max)
                                i = v.tensor_tensor(
                                    out=mneg[:, 0:T], in0=mneg[:, 0:T],
                                    in1=u32[:], op=OP.min)
                    ti(i, "gV")
                app_v = cur("gV")
                if not quant:
                    return app_v

                @block.vector
                def _(v):
                    v.tensor_scalar(out=u32[:], in0=mneg[:, 0:T], scalar1=-1.0,
                                    scalar2=None, op0=OP.mult)
                    ti(v.tensor_tensor(out=mpos[:, 0:T], in0=mpos[:, 0:T],
                                       in1=u32[:], op=OP.max), "gV")
                am_v = cur("gV")

                @block.tensor
                def _(t):
                    t.wait_ge(SEM("gV"), am_v)
                    for j in range(8):
                        i = t.transpose(PS_D[:, j * 128:(j + 1) * 128],
                                        mpos[:, j * 128:(j + 1) * 128],
                                        ident[:])
                    ti(i, "gT")
                tp_t = cur("gT")

                @block.vector
                def _(v):
                    v.wait_ge(SEM("gT"), tp_t)
                    for j in range(8):
                        v.tensor_reduce(out=st_tm[:, j:j + 1],
                                        in_=PS_D[:, j * 128:(j + 1) * 128],
                                        axis=AX, op=OP.max)
                    vpad(v)
                    v.tensor_scalar(out=st_tm[:, 8:16], in0=st_tm[:, 0:8],
                                    scalar1=1e-5, scalar2=None, op0=OP.max)
                    vpad(v)
                    v.reciprocal(out=st_tm[:, 0:8], in_=st_tm[:, 8:16])
                    vpad(v)
                    ti(v.tensor_scalar(out=st_tm[:, 0:8], in0=st_tm[:, 0:8],
                                       scalar1=127.0, scalar2=None,
                                       op0=OP.mult), "gV")
                bcast_tm(list(range(8)), bc_f)
                qs_v = cur("gV")

                @block.vector
                def _(v):
                    v.wait_ge(SEM("gV"), qs_v)
                    v.tensor_tensor(out=bc_r[:], in0=bc_r[:], in1=bc_f[:],
                                    op=OP.mult)
                    v.tensor_tensor(out=bc_b[:], in0=bc_b[:], in1=bc_f[:],
                                    op=OP.mult)
                    for c in range(DC):
                        v.tensor_tensor(out=u32[:],
                                        in0=x_sb[:, c * T:(c + 1) * T],
                                        in1=bc_r[:], op=OP.mult)
                        v.tensor_tensor(out=u32[:], in0=u32[:], in1=bc_b[:],
                                        op=OP.subtract)
                        v.tensor_scalar(out=uh16[:], in0=u32[:], scalar1=RB,
                                        scalar2=None, op0=OP.add)
                        v.tensor_scalar(out=u32[:], in0=uh16[:], scalar1=RB,
                                        scalar2=127.0, op0=OP.subtract,
                                        op1=OP.min)
                        i = v.tensor_scalar(out=lnx[:, c * T:(c + 1) * T],
                                            in0=u32[:], scalar1=-127.0,
                                            scalar2=None, op0=OP.max)
                    ti(i, "gV")
                return cur("gV")

            def ffn(first, lnx, h_sb, mpos, mneg, tstg, nx_ready):
                tern_src_up = None if first else tern_up_d
                xsrc = x_in if first else xres_d
                final = (not first) and upto >= 3

                qUP0, mUP0, eUP0, hUP0 = (cur("qUP"), cur("mUP"), cur("eUP"),
                                          cur("hUP"))
                dWU0, dTU0 = cur("dWU"), cur("dTU")

                if first:
                    @block.gpsimd
                    def _(g):
                        def tstore(j):
                            g.wait_ge(SEM("qUP"), qUP0 + j + 1)
                            ti(g.dma_start(out=tern_up_d[j],
                                           in_=tstg[j % 2][:, 0:2048]),
                               "dTU", 16)

                        for i in range(NUP):
                            if i >= 2:
                                tstore(i - 2)
                                g.wait_ge(SEM("qUP"), qUP0 + i - 1)
                            else:
                                g.wait_ge(SEM("aW"), NUP + NDN)
                            ti(g.dma_start(
                                out=wstg[i % 2][:, 0:2048].rearrange(
                                    "p (c o) -> p c o", o=128),
                                in_=wupT[:, i * 128:(i + 1) * 128].rearrange(
                                    "(c p) o -> p c o", p=128)), "dWU", 16)
                        tstore(NUP - 2)
                        tstore(NUP - 1)

                    def up_quant(v, i):
                        v.wait_ge(SEM("dWU"), dWU0 + (i + 1) * 16)
                        if i >= 2:
                            v.wait_ge(SEM("mUP"), mUP0 + i - 1)
                            v.wait_ge(SEM("dTU"), dTU0 + (i - 1) * 16)
                        v.tensor_scalar(out=babs[:, 0:2048],
                                        in0=wstg[i % 2][:, 0:2048],
                                        scalar1=sc[:, 1:2], scalar2=None,
                                        op0=OP.is_le)
                        ti(v.scalar_tensor_tensor(
                            out=tstg[i % 2][:, 0:2048],
                            in0=wstg[i % 2][:, 0:2048],
                            scalar=sc[:, 0:1], in1=babs[:, 0:2048],
                            op0=OP.is_ge, op1=OP.subtract), "qUP")

                else:
                    @block.gpsimd
                    def _(g):
                        for i in range(NUP):
                            if i >= 2:
                                g.wait_ge(SEM("mUP"), mUP0 + i - 1)
                            ti(g.dma_start(out=tstg[i % 2][:, 0:2048],
                                           in_=tern_up_d[i]), "dTU", 16)

                @block.tensor
                def _(t):
                    t.wait_ge(SEM("gV"), nx_ready)
                    for i in range(NUP):
                        if first:
                            t.wait_ge(SEM("qUP"), qUP0 + i + 1)
                        else:
                            t.wait_ge(SEM("dTU"), dTU0 + (i + 1) * 16)
                        if i >= 2:
                            t.wait_ge(SEM("eUP"), eUP0 + i - 1)
                        pp = (PS_A, PS_B)[i % 2]
                        for c in range(DC):
                            t.matmul(pp[:, 0:512],
                                     tstg[i % 2][:, c * 128:(c + 1) * 128],
                                     lnx[:, c * T:c * T + 512],
                                     start=(c == 0), stop=(c == DC - 1))
                            i2 = t.matmul(
                                pp[:, 512:1024],
                                tstg[i % 2][:, c * 128:(c + 1) * 128],
                                lnx[:, c * T + 512:(c + 1) * T],
                                start=(c == 0), stop=(c == DC - 1))
                        ti(i2, "mUP")

                @block.scalar
                def _(s_):
                    for i in range(NUP):
                        s_.wait_ge(SEM("mUP"), mUP0 + i + 1)
                        pp = (PS_A, PS_B)[i % 2]
                        ti(s_.activation(out=h_sb[:, i * T:(i + 1) * T],
                                         in_=pp[:, 0:T],
                                         func=AF.Relu, scale=0.125), "eUP")

                def up_hsq(v, i):
                    v.wait_ge(SEM("eUP"), eUP0 + i + 1)
                    v.tensor_tensor(out=u32[:], in0=h_sb[:, i * T:(i + 1) * T],
                                    in1=h_sb[:, i * T:(i + 1) * T],
                                    op=OP.mult)
                    if i == 0:
                        i2 = v.tensor_copy(out=mpos[:, 0:T], in_=u32[:])
                    else:
                        i2 = v.tensor_tensor(
                            out=mpos[:, 0:T], in0=mpos[:, 0:T],
                            in1=u32[:], op=OP.max)
                    ti(i2, "hUP")

                @block.vector
                def _(v):
                    if first:
                        v.wait_ge(SEM("gV"), sc_v)
                        for i in range(NUP):
                            up_quant(v, i)
                            if i >= 2:
                                up_hsq(v, i - 2)
                        up_hsq(v, NUP - 2)
                        up_hsq(v, NUP - 1)
                    else:
                        for i in range(NUP):
                            up_hsq(v, i)
                hmax_v = cur("hUP")

                @block.tensor
                def _(t):
                    t.wait_ge(SEM("hUP"), hmax_v)
                    for j in range(8):
                        i = t.transpose(PS_D[:, j * 128:(j + 1) * 128],
                                        mpos[:, j * 128:(j + 1) * 128],
                                        ident[:])
                    ti(i, "gT")
                tph = cur("gT")

                @block.vector
                def _(v):
                    v.wait_ge(SEM("gT"), tph)
                    for j in range(8):
                        v.tensor_reduce(out=st_tm[:, j:j + 1],
                                        in_=PS_D[:, j * 128:(j + 1) * 128],
                                        axis=AX, op=OP.max)
                    v.tensor_scalar(out=st_tm[:, 24:32], in0=st_tm[:, 8:16],
                                    scalar1=sc[:, 4:5], scalar2=8.0 / 127.0,
                                    op0=OP.mult, op1=OP.mult)
                    vpad(v)
                    v.tensor_tensor(out=st_tm[:, 24:32], in0=st_tm[:, 24:32],
                                    in1=st_tm[:, 24:32], op=OP.mult)
                    vpad(v)
                    v.tensor_tensor(out=st_tm[:, 16:24], in0=st_tm[:, 24:32],
                                    in1=st_tm[:, 0:8], op=OP.mult)
                    vpad(v)
                    v.tensor_scalar(out=st_tm[:, 16:24], in0=st_tm[:, 16:24],
                                    scalar1=1e-5, scalar2=None, op0=OP.max)
                    vpad(v)
                    v.reciprocal(out=st_tm[:, 0:8], in_=st_tm[:, 16:24])
                    vpad(v)
                    v.tensor_scalar(out=st_tm[:, 0:8], in0=st_tm[:, 0:8],
                                    scalar1=127.0, scalar2=None, op0=OP.mult)
                    vpad(v)
                    ti(v.tensor_tensor(out=st_tm[:, 0:8], in0=st_tm[:, 0:8],
                                       in1=st_tm[:, 24:32], op=OP.mult), "gV")
                bcast_tm(list(range(8)), bc_b)

                @block.vector
                def _(v):
                    ti(v.tensor_scalar(out=st_tm[:, 16:24],
                                       in0=st_tm[:, 16:24],
                                       scalar1=sc[:, 5:6], scalar2=0.5 / 127.0,
                                       op0=OP.mult, op1=OP.mult), "gV")
                bcast_tm(list(range(16, 24)), bc_f)
                fb_v = cur("gV")

                @block.vector
                def _(v):
                    v.wait_ge(SEM("gV"), fb_v)
                    for i in range(OCH):
                        v.tensor_tensor(out=u32[:],
                                        in0=h_sb[:, i * T:(i + 1) * T],
                                        in1=h_sb[:, i * T:(i + 1) * T],
                                        op=OP.mult)
                        v.tensor_tensor(out=u32[:], in0=u32[:],
                                        in1=bc_b[:], op=OP.mult)
                        v.tensor_scalar(out=uh16[:], in0=u32[:], scalar1=RB,
                                        scalar2=None, op0=OP.add)
                        i2 = v.tensor_scalar(out=h_sb[:, i * T:(i + 1) * T],
                                             in0=uh16[:], scalar1=RB,
                                             scalar2=127.0, op0=OP.subtract,
                                             op1=OP.min)
                    ti(i2, "gV")
                nh_v = cur("gV")

                qDN0, mDN0, eDN0 = cur("qDN"), cur("mDN"), cur("eDN")
                dWD0, dTD0, dXC0, dXO0 = (cur("dWD"), cur("dTD"), cur("dXC"),
                                          cur("dXO"))
                tag = "a" if first else "b"
                xc = [alloc(f"xc{i}{tag}", [128, T]) for i in range(2)]

                @block.sync
                def _(sy):
                    def xstore(j):
                        sy.wait_ge(SEM("eDN"), eDN0 + 2 * (j + 1))
                        ti(sy.dma_start(
                            out=out_fm[j * 128:(j + 1) * 128, :] if final
                            else xres_d[:, j * T:(j + 1) * T],
                            in_=xc[j % 2][:]), "dXO", 16)

                    for i in range(NDN):
                        if i >= 2:
                            xstore(i - 2)
                            sy.wait_ge(SEM("eDN"), eDN0 + 2 * (i - 1))
                            sy.wait_ge(SEM("dXO"), dXO0 + (i - 1) * 16)
                        ti(sy.dma_start(
                            out=xc[i % 2][:],
                            in_=xsrc[i * 128:(i + 1) * 128, :] if first
                            else xsrc[:, i * T:(i + 1) * T]), "dXC", 16)
                    xstore(NDN - 2)
                    xstore(NDN - 1)

                if first:
                    @block.gpsimd
                    def _(g):
                        def tstored(j):
                            g.wait_ge(SEM("qDN"), qDN0 + j + 1)
                            ti(g.dma_start(out=tern_dn_d[j],
                                           in_=tstg[j % 2][:, 0:4096]),
                               "dTD", 16)

                        for i in range(NDN):
                            if i >= 2:
                                tstored(i - 2)
                                g.wait_ge(SEM("qDN"), qDN0 + i - 1)
                            else:
                                g.wait_ge(SEM("qUP"), qUP0 + NUP)
                            ti(g.dma_start(
                                out=wstg[i % 2][:, 0:4096].rearrange(
                                    "p (c o) -> p c o", o=128),
                                in_=wdownT[:, i * 128:(i + 1) * 128].rearrange(
                                    "(c p) o -> p c o", p=128)), "dWD", 16)
                        tstored(NDN - 2)
                        tstored(NDN - 1)

                    def dn_quant(v, i):
                        v.wait_ge(SEM("dWD"), dWD0 + (i + 1) * 16)
                        if i >= 2:
                            v.wait_ge(SEM("mDN"), mDN0 + i - 1)
                            v.wait_ge(SEM("dTD"), dTD0 + (i - 1) * 16)
                        else:
                            v.wait_ge(SEM("mUP"), mUP0 + NUP)
                        v.tensor_scalar(out=babs[:, 0:4096],
                                        in0=wstg[i % 2][:, 0:4096],
                                        scalar1=sc[:, 3:4], scalar2=None,
                                        op0=OP.is_le)
                        ti(v.scalar_tensor_tensor(
                            out=tstg[i % 2][:, 0:4096],
                            in0=wstg[i % 2][:, 0:4096],
                            scalar=sc[:, 2:3], in1=babs[:, 0:4096],
                            op0=OP.is_ge, op1=OP.subtract), "qDN")

                else:
                    @block.gpsimd
                    def _(g):
                        for i in range(NDN):
                            if i >= 2:
                                g.wait_ge(SEM("mDN"), mDN0 + i - 1)
                            else:
                                g.wait_ge(SEM("mUP"), mUP0 + NUP)
                            ti(g.dma_start(out=tstg[i % 2][:, 0:4096],
                                           in_=tern_dn_d[i]), "dTD", 16)

                @block.tensor
                def _(t):
                    t.wait_ge(SEM("gV"), nh_v)
                    for i in range(NDN):
                        if first:
                            t.wait_ge(SEM("qDN"), qDN0 + i + 1)
                        else:
                            t.wait_ge(SEM("dTD"), dTD0 + (i + 1) * 16)
                        if i >= 2:
                            t.wait_ge(SEM("eDN"), eDN0 + 2 * (i - 1))
                        pp = (PS_A, PS_B)[i % 2]
                        for c in range(OCH):
                            t.matmul(pp[:, 0:512],
                                     tstg[i % 2][:, c * 128:(c + 1) * 128],
                                     h_sb[:, c * T:c * T + 512],
                                     start=(c == 0), stop=(c == OCH - 1))
                            i2 = t.matmul(
                                pp[:, 512:1024],
                                tstg[i % 2][:, c * 128:(c + 1) * 128],
                                h_sb[:, c * T + 512:(c + 1) * T],
                                start=(c == 0), stop=(c == OCH - 1))
                        ti(i2, "mDN")

                def dn_evac(v, i):
                    v.wait_ge(SEM("mDN"), mDN0 + i + 1)
                    v.wait_ge(SEM("dXC"), dXC0 + (i + 1) * 16)
                    pp = (PS_A, PS_B)[i % 2]
                    ti(v.tensor_tensor(out=u32[:], in0=pp[:, 0:T],
                                       in1=bc_f[:], op=OP.mult), "eDN")
                    ti(v.tensor_tensor(out=xc[i % 2][:], in0=u32[:],
                                       in1=xc[i % 2][:], op=OP.add), "eDN")

                @block.vector
                def _(v):
                    if first:
                        v.wait_ge(SEM("gV"), sc_v)
                        for i in range(NDN):
                            dn_quant(v, i)
                            if i >= 2:
                                dn_evac(v, i - 2)
                        dn_evac(v, NDN - 2)
                        dn_evac(v, NDN - 1)
                    else:
                        for i in range(NDN):
                            dn_evac(v, i)

                xo_dma = cur("dXO")
                sync_all()
                free(f"xc0{tag}", f"xc1{tag}")
                return xo_dma

            def attention():
                nonlocal x_sb
                x_sb = alloc("x_sb2", [128, DC * T], side="right")

                @block.sync
                def _(sy):
                    ti(sy.dma_start(out=x_sb[:], in_=xres_d), "dCN", 16)
                x1_dma = cur("dCN")

                @block.vector
                def _(v):
                    v.wait_ge(SEM("dCN"), x1_dma)
                    v.sem_inc(SEM("gV"), 1)
                vals["gV"] += 1

                wqr_bf = alloc("wqr_bf", [128, RD], BF16)
                wkr_bf = alloc("wkr_bf", [128, RD], BF16)
                cq = alloc("cq", [128, QCC * T], BF16)
                ckv = alloc("ckv", [128, KVCC * S], BF16)
                y = act16
                y_v = ln_phase(y, False, None, None)

                wdkv_bf = alloc("wdkv_bf", [128, DC * KVC], BF16)
                wdqb = [alloc(f"wdqb{i}", [128, DC * 128], BF16)
                        for i in range(2)]

                @block.gpsimd
                def _(g):
                    ti(g.dma_start(
                        out=wdkv_bf[:].rearrange("p (c k) -> p c k", k=KVC),
                        in_=wdkv.rearrange("(c p) k -> p c k", p=128)),
                       "dAW", 16)
                    ti(g.dma_start(out=wqr_bf[:], in_=wqr), "dAW", 16)
                    ti(g.dma_start(out=wkr_bf[:], in_=wkr), "dAW", 16)
                dkv_w = cur("dAW")

                ckv_st = alloc("ckv_st", [128, KVCC * T], BF16)

                dkv_ev0 = cur("eAT")
                for kc in range(KVCC):
                    @block.tensor
                    def _(t, kc=kc):
                        if kc == 0:
                            t.wait_ge(SEM("gV"), y_v)
                            t.wait_ge(SEM("dAW"), dkv_w)
                        else:
                            t.wait_ge(SEM("eAT"), dkv_ev0 + kc)
                        for ic in range(DC):
                            t.matmul(PS_A[:, 0:512],
                                     wdkv_bf[:, ic * KVC + kc * 128:
                                             ic * KVC + (kc + 1) * 128],
                                     y[:, ic * T:ic * T + 512],
                                     start=(ic == 0), stop=(ic == DC - 1))
                            i2 = t.matmul(PS_A[:, 512:1024],
                                          wdkv_bf[:, ic * KVC + kc * 128:
                                                  ic * KVC + (kc + 1) * 128],
                                          y[:, ic * T + 512:(ic + 1) * T],
                                          start=(ic == 0), stop=(ic == DC - 1))
                        ti(i2, "mAT")

                    @block.scalar
                    def _(s_, kc=kc, tgt=cur("mAT")):
                        s_.wait_ge(SEM("mAT"), tgt)
                        ti(s_.activation(out=ckv_st[:, kc * T:(kc + 1) * T],
                                         in_=PS_A[:, 0:T], func=AF.Identity,
                                         bias=bdkv_sb[:, kc:kc + 1]), "eAT")
                ckv_s = cur("eAT")

                @block.gpsimd
                def _(g):
                    g.wait_ge(SEM("eAT"), ckv_s)
                    ti(g.dma_start(
                        out=cc_in.rearrange("(p f) -> p f", p=128),
                        in_=ckv_st[:]), "dCC", 16)
                    g.wait_ge(SEM("dCC"), cur("dCC"))
                    ti(g.collective_compute(
                        "AllGather", OP.bypass,
                        replica_groups=[[0, 1], [2, 3], [4, 5], [6, 7]],
                        ins=[cc_in], outs=[cc_out.rearrange("a b -> (a b)")]),
                       "cCC", 1)
                cc_done = cur("cCC")

                @block.sync
                def _(sy):
                    sy.wait_ge(SEM("cCC"), cc_done)
                    for half in range(2):
                        src = cc_out[half].rearrange("(p f) -> p f", p=128)
                        for kc in range(KVCC):
                            ti(sy.dma_start(
                                out=ckv[:, kc * S + half * T:
                                        kc * S + (half + 1) * T],
                                in_=src[:, kc * T:(kc + 1) * T]), "dCN", 16)
                ckv_ready = cur("dCN")

                mDQ0 = cur("mAT")
                dAW_dq0 = cur("dAW")

                @block.gpsimd
                def _(g):
                    for i in range(QCC):
                        if i >= 2:
                            g.wait_ge(SEM("mAT"), mDQ0 + i - 1)
                        ti(g.dma_start(
                            out=wdqb[i % 2][:].rearrange(
                                "p (c o) -> p c o", o=128),
                            in_=wdq[:, i * 128:(i + 1) * 128].rearrange(
                                "(c p) o -> p c o", p=128)), "dAW", 16)

                dq_ev0 = cur("eAT")
                for qc in range(QCC):
                    @block.tensor
                    def _(t, qc=qc):
                        t.wait_ge(SEM("dAW"), dAW_dq0 + (qc + 1) * 16)
                        if qc >= 1:
                            t.wait_ge(SEM("eAT"), dq_ev0 + qc)
                        for ic in range(DC):
                            t.matmul(PS_B[:, 0:512],
                                     wdqb[qc % 2][:, ic * 128:(ic + 1) * 128],
                                     y[:, ic * T:ic * T + 512],
                                     start=(ic == 0), stop=(ic == DC - 1))
                            i2 = t.matmul(PS_B[:, 512:1024],
                                          wdqb[qc % 2][:, ic * 128:(ic + 1) * 128],
                                          y[:, ic * T + 512:(ic + 1) * T],
                                          start=(ic == 0), stop=(ic == DC - 1))
                        ti(i2, "mAT")

                    @block.scalar
                    def _(s_, qc=qc, tgt=cur("mAT")):
                        s_.wait_ge(SEM("mAT"), tgt)
                        ti(s_.activation(out=cq[:, qc * T:(qc + 1) * T],
                                         in_=PS_B[:, 0:T], func=AF.Identity,
                                         bias=bdq_sb[:, qc:qc + 1]), "eAT")

                sync_all()
                free("wdkv_bf", "wdqb0", "wdqb1", "ckv_st")

                cosq_sb = alloc("cosq_sb", [RD // 2, T], BF16)
                sinq_sb = alloc("sinq_sb", [RD // 2, T], BF16)
                cosk_sb = alloc("cosk_sb", [RD // 2, S], BF16)
                sink_sb = alloc("sink_sb", [RD // 2, S], BF16)
                buvb_sb = alloc("buvb_sb", [128, DM], BF16)

                @block.gpsimd
                def _(g):
                    ti(g.dma_start(out=buvb_sb[:], in_=buv_b), "dAW", 16)
                    ti(g.dma_start(out=cosq_sb[:], in_=cosq), "dAW", 16)
                    ti(g.dma_start(out=sinq_sb[:], in_=sinq), "dAW", 16)
                    ti(g.dma_start(out=cosk_sb[:], in_=cosk), "dAW", 16)
                    ti(g.dma_start(out=sink_sb[:], in_=sink), "dAW", 16)
                tbl_d = cur("dAW")

                @block.vector
                def _(v):
                    v.wait_ge(SEM("dAW"), tbl_d)
                    v.sem_inc(SEM("vAT"), 1)
                vals["vAT"] += 1

                ctx = act16
                k_h = alloc("k_h", [128, S], BF16)
                v_h = alloc("v_h", [128, S], BF16)
                q_h = alloc("q_h", [128, T], BF16)
                scr2k = alloc("scr2k", [128, S], BF16)
                kslb = [alloc(f"kslb{i}", [128, KVCC * 128], BF16)
                        for i in range(2)]
                vslb = [alloc(f"vslb{i}", [128, KVCC * 128], BF16)
                        for i in range(2)]
                qslb = [alloc(f"qslb{i}", [128, QCC * 128], BF16)
                        for i in range(2)]

                head_done = [cur("vAT")]
                slc_used = []
                prev_inv = cur("eAT")

                for h in range(H):
                    @block.gpsimd
                    def _(g, h=h):
                        if h >= 2:
                            g.wait_ge(SEM("mAT"), slc_used[h - 2])
                        for w_, dst in ((wuk, kslb[h % 2]), (wuv, vslb[h % 2]),
                                        (wuq, qslb[h % 2])):
                            ti(g.dma_start(
                                out=dst[:].rearrange("p (c o) -> p c o", o=128),
                                in_=w_[:, h * 128:(h + 1) * 128].rearrange(
                                    "(c p) o -> p c o", p=128)), "dAW", 16)
                    slw = cur("dAW")

                    a_ev0 = cur("eAT")
                    a_mm0 = cur("mAT")
                    pi = prev_inv
                    hd_prev = head_done[h]

                    @block.tensor
                    def _(t, h=h, slw=slw, a_ev0=a_ev0, pi=pi, hd=hd_prev):
                        t.wait_ge(SEM("dAW"), slw)
                        t.wait_ge(SEM("eAT"), pi)
                        t.wait_ge(SEM("vAT"), hd)
                        if h == 0:
                            t.wait_ge(SEM("dCN"), ckv_ready)
                        for q4 in range(4):
                            if q4 >= 2:
                                t.wait_ge(SEM("eAT"), a_ev0 + q4 - 1)
                            reg = PS_A[:, (q4 % 2) * 512:(q4 % 2) * 512 + 512]
                            for kc in range(KVCC):
                                i2 = t.matmul(
                                    reg,
                                    kslb[h % 2][:, kc * 128:(kc + 1) * 128],
                                    ckv[:, kc * S + q4 * 512:
                                        kc * S + (q4 + 1) * 512],
                                    start=(kc == 0), stop=(kc == KVCC - 1))
                            ti(i2, "mAT")

                    @block.scalar
                    def _(s_, h=h, a_mm0=a_mm0):
                        for q4 in range(4):
                            s_.wait_ge(SEM("mAT"), a_mm0 + q4 + 1)
                            ti(s_.activation(
                                out=k_h[:, q4 * 512:(q4 + 1) * 512],
                                in_=PS_A[:, (q4 % 2) * 512:(q4 % 2) * 512 + 512],
                                func=AF.Identity, bias=buk_sb[:, h:h + 1]), "eAT")
                    k_done = cur("eAT")

                    v_mm0 = cur("mAT")
                    v_ev0 = cur("vAT")

                    @block.tensor
                    def _(t, h=h, k_done=k_done, v_ev0=v_ev0):
                        t.wait_ge(SEM("eAT"), k_done)
                        banks = [PS_B[:, 0:512], PS_B[:, 512:1024],
                                 PS_D[:, 0:512], PS_D[:, 512:1024]]
                        for skc in range(SKC):
                            if skc >= 4:
                                t.wait_ge(SEM("vAT"), v_ev0 + skc - 3)
                            reg = banks[skc % 4][:, 0:128]
                            for kc in range(KVCC):
                                i2 = t.matmul(
                                    reg,
                                    ckv[:, kc * S + skc * 128:
                                        kc * S + (skc + 1) * 128],
                                    vslb[h % 2][:, kc * 128:(kc + 1) * 128],
                                    start=(kc == 0), stop=(kc == KVCC - 1))
                            ti(i2, "mAT")

                    @block.vector
                    def _(v, h=h, v_mm0=v_mm0):
                        banks = [PS_B[:, 0:512], PS_B[:, 512:1024],
                                 PS_D[:, 0:512], PS_D[:, 512:1024]]
                        for skc in range(SKC):
                            v.wait_ge(SEM("mAT"), v_mm0 + skc + 1)
                            ti(v.tensor_tensor(
                                out=v_h[:, skc * 128:(skc + 1) * 128],
                                in0=banks[skc % 4][:, 0:128],
                                in1=buvb_sb[:, h * 128:(h + 1) * 128],
                                op=OP.add), "vAT")
                    v_done = cur("vAT")

                    @block.tensor
                    def _(t, k_done=k_done, v_done=v_done):
                        t.wait_ge(SEM("eAT"), k_done)
                        t.wait_ge(SEM("vAT"), v_done)
                        t.matmul(PS_C[0:RD, 0:512], wkr_bf[:, 0:RD],
                                 k_h[:, 0:512], start=True, stop=True)
                        ti(t.matmul(PS_C[0:RD, 512:1024], wkr_bf[:, 0:RD],
                                    k_h[:, 512:1024], start=True, stop=True),
                           "mAT")
                    kr0 = cur("mAT")

                    @block.vector
                    def _(v, kr0=kr0):
                        v.wait_ge(SEM("mAT"), kr0)
                        v.tensor_tensor(out=u32[0:32, 0:T], in0=PS_C[0:32, 0:T],
                                        in1=cosk_sb[:, 0:T], op=OP.mult)
                        v.tensor_tensor(out=bc_f[0:32, 0:T],
                                        in0=PS_C[32:64, 0:T],
                                        in1=sink_sb[:, 0:T], op=OP.mult)
                        ti(v.tensor_tensor(out=k_h[0:32, 0:T],
                                           in0=u32[0:32, 0:T],
                                           in1=bc_f[0:32, 0:T],
                                           op=OP.subtract), "vAT")
                    krv0 = cur("vAT")

                    @block.tensor
                    def _(t, krv0=krv0):
                        t.wait_ge(SEM("vAT"), krv0)
                        t.matmul(PS_C[0:RD, 0:512], wkr_bf[:, 0:RD],
                                 k_h[:, T:T + 512], start=True, stop=True)
                        ti(t.matmul(PS_C[0:RD, 512:1024], wkr_bf[:, 0:RD],
                                    k_h[:, T + 512:2 * T], start=True,
                                    stop=True), "mAT")
                    kr1 = cur("mAT")

                    @block.vector
                    def _(v, kr1=kr1):
                        v.wait_ge(SEM("mAT"), kr1)
                        v.tensor_tensor(out=u32[0:32, 0:T], in0=PS_C[0:32, 0:T],
                                        in1=cosk_sb[:, T:2 * T], op=OP.mult)
                        v.tensor_tensor(out=bc_f[0:32, 0:T],
                                        in0=PS_C[32:64, 0:T],
                                        in1=sink_sb[:, T:2 * T], op=OP.mult)
                        v.tensor_tensor(out=k_h[0:32, T:2 * T],
                                        in0=u32[0:32, 0:T], in1=bc_f[0:32, 0:T],
                                        op=OP.subtract)
                        v.memset(k_h[32:64, 0:S], 0.0)
                        ti(v.memset(k_h[32:33, 0:S], 1.0), "vAT")
                    k_rope = cur("vAT")

                    @block.scalar
                    def _(s_, k_rope=k_rope):
                        s_.wait_ge(SEM("vAT"), k_rope)
                        ti(s_.activation(out=scr2k[:, 0:S], in_=k_h[:, 0:S],
                                         func=AF.Square), "eAT")
                    ksq_s = cur("eAT")

                    @block.tensor
                    def _(t, ksq_s=ksq_s):
                        t.wait_ge(SEM("eAT"), ksq_s)
                        t.matmul(PS_C[0:1, 0:512], ones_bf[:, 0:1],
                                 scr2k[:, 0:512], start=True, stop=True)
                        t.matmul(PS_C[0:1, 512:1024], ones_bf[:, 0:1],
                                 scr2k[:, 512:1024], start=True, stop=True)
                        t.matmul(PS_D[0:1, 0:512], ones_bf[:, 0:1],
                                 scr2k[:, 1024:1536], start=True, stop=True)
                        ti(t.matmul(PS_D[0:1, 512:1024], ones_bf[:, 0:1],
                                    scr2k[:, 1536:2048], start=True,
                                    stop=True), "mAT")
                    kn_t = cur("mAT")

                    @block.vector
                    def _(v, kn_t=kn_t):
                        v.wait_ge(SEM("mAT"), kn_t)
                        v.tensor_reduce(out=tiny[0:1, 3:4],
                                        in_=PS_C[0:1, 0:T], axis=AX, op=OP.max)
                        v.tensor_reduce(out=tiny[0:1, 4:5],
                                        in_=PS_D[0:1, 0:T], axis=AX, op=OP.max)
                        vpad(v)
                        ti(v.tensor_tensor(out=tiny[0:1, 2:3],
                                           in0=tiny[0:1, 3:4],
                                           in1=tiny[0:1, 4:5], op=OP.max),
                           "vAT")
                    k2m_v = cur("vAT")

                    q_mm0 = cur("mAT")

                    @block.tensor
                    def _(t, h=h, k2m_v=k2m_v):
                        t.wait_ge(SEM("vAT"), k2m_v)
                        for qc in range(QCC):
                            t.matmul(PS_A[:, 0:512],
                                     qslb[h % 2][:, qc * 128:(qc + 1) * 128],
                                     cq[:, qc * T:qc * T + 512],
                                     start=(qc == 0), stop=(qc == QCC - 1))
                            i2 = t.matmul(
                                PS_A[:, 512:1024],
                                qslb[h % 2][:, qc * 128:(qc + 1) * 128],
                                cq[:, qc * T + 512:(qc + 1) * T],
                                start=(qc == 0), stop=(qc == QCC - 1))
                        ti(i2, "mAT")
                    q_mm = cur("mAT")
                    slc_used.append(q_mm)

                    @block.scalar
                    def _(s_, h=h, q_mm=q_mm):
                        s_.wait_ge(SEM("mAT"), q_mm)
                        ti(s_.activation(out=q_h[:], in_=PS_A[:, 0:T],
                                         func=AF.Identity, scale=ISD,
                                         bias=buq_sb[:, h:h + 1]), "eAT")
                    q_ev = cur("eAT")

                    @block.tensor
                    def _(t, q_ev=q_ev):
                        t.wait_ge(SEM("eAT"), q_ev)
                        t.matmul(PS_D[0:RD, 0:512], wqr_bf[:, 0:RD],
                                 q_h[:, 0:512], start=True, stop=True)
                        ti(t.matmul(PS_D[0:RD, 512:1024], wqr_bf[:, 0:RD],
                                    q_h[:, 512:1024], start=True, stop=True),
                           "mAT")
                    qr_t = cur("mAT")

                    @block.vector
                    def _(v, qr_t=qr_t):
                        v.wait_ge(SEM("mAT"), qr_t)
                        v.tensor_tensor(out=u32[0:32, 0:T], in0=PS_D[0:32, 0:T],
                                        in1=cosq_sb[:], op=OP.mult)
                        v.tensor_tensor(out=bc_f[0:32, 0:T],
                                        in0=PS_D[32:64, 0:T],
                                        in1=sinq_sb[:], op=OP.mult)
                        v.tensor_tensor(out=q_h[0:32, :], in0=u32[0:32, 0:T],
                                        in1=bc_f[0:32, 0:T], op=OP.subtract)
                        ti(v.memset(q_h[32:64, :], 0.0), "vAT")
                    q_rope = cur("vAT")

                    @block.scalar
                    def _(s_, q_rope=q_rope):
                        s_.wait_ge(SEM("vAT"), q_rope)
                        ti(s_.activation(out=scr2k[:, 0:T], in_=q_h[:],
                                         func=AF.Square), "eAT")
                    qsq_s = cur("eAT")

                    @block.tensor
                    def _(t, qsq_s=qsq_s):
                        t.wait_ge(SEM("eAT"), qsq_s)
                        t.matmul(PS_C[0:1, 0:512], ones_bf[:, 0:1],
                                 scr2k[:, 0:512], start=True, stop=True)
                        ti(t.matmul(PS_C[0:1, 512:1024], ones_bf[:, 0:1],
                                    scr2k[:, 512:1024], start=True, stop=True),
                           "mAT")
                    qn_t = cur("mAT")

                    @block.vector
                    def _(v, qn_t=qn_t):
                        v.wait_ge(SEM("mAT"), qn_t)
                        ti(v.tensor_scalar(out=srow16[0:1, :],
                                           in0=PS_C[0:1, 0:T],
                                           scalar1=tiny[0:1, 2:3],
                                           scalar2=-0.505,
                                           op0=OP.add, op1=OP.mult), "vAT")
                    negm_v = cur("vAT")

                    @block.gpsimd
                    def _(g, negm_v=negm_v):
                        g.wait_ge(SEM("vAT"), negm_v)
                        ti(g.dma_start(out=q_h[32:33, :], in_=srow16[0:1, :]),
                           "dRW", 16)
                    row_d = cur("dRW")

                    mST0, eEX0, mCX0 = cur("mST"), cur("eEX"), cur("mCX")

                    @block.tensor
                    def _(t, row_d=row_d, v_done=v_done,
                          mST0=mST0, eEX0=eEX0, mCX0=mCX0):
                        t.wait_ge(SEM("dRW"), row_d)
                        t.wait_ge(SEM("vAT"), v_done)
                        for skc in range(SKC):
                            pp = (PS_A, PS_D)[skc % 2]
                            if skc >= 2:
                                t.wait_ge(SEM("eEX"), eEX0 + skc - 1)
                            t.matmul(pp[:, 0:512],
                                     k_h[:, skc * 128:(skc + 1) * 128],
                                     q_h[:, 0:512], start=True, stop=True)
                            ti(t.matmul(pp[:, 512:1024],
                                        k_h[:, skc * 128:(skc + 1) * 128],
                                        q_h[:, 512:1024], start=True,
                                        stop=True), "mST")
                            if skc >= 1:
                                c = skc - 1
                                t.wait_ge(SEM("eEX"), eEX0 + c + 1)
                                pb = scr2k[:, (c % 2) * T:(c % 2 + 1) * T]
                                t.matmul(PS_C[0:1, 0:512], ones_bf[:, 0:1],
                                         pb[:, 0:512], start=(c == 0),
                                         stop=(c == SKC - 1))
                                t.matmul(PS_C[0:1, 512:1024], ones_bf[:, 0:1],
                                         pb[:, 512:1024], start=(c == 0),
                                         stop=(c == SKC - 1))
                                t.matmul(PS_B[:, 0:512],
                                         v_h[:, c * 128:(c + 1) * 128],
                                         pb[:, 0:512], start=(c == 0),
                                         stop=(c == SKC - 1))
                                ti(t.matmul(PS_B[:, 512:1024],
                                            v_h[:, c * 128:(c + 1) * 128],
                                            pb[:, 512:1024], start=(c == 0),
                                            stop=(c == SKC - 1)), "mCX")
                        c = SKC - 1
                        t.wait_ge(SEM("eEX"), eEX0 + c + 1)
                        pb = scr2k[:, (c % 2) * T:(c % 2 + 1) * T]
                        t.matmul(PS_C[0:1, 0:512], ones_bf[:, 0:1],
                                 pb[:, 0:512], start=False, stop=True)
                        t.matmul(PS_C[0:1, 512:1024], ones_bf[:, 0:1],
                                 pb[:, 512:1024], start=False, stop=True)
                        t.matmul(PS_B[:, 0:512],
                                 v_h[:, c * 128:(c + 1) * 128],
                                 pb[:, 0:512], start=False, stop=True)
                        ti(t.matmul(PS_B[:, 512:1024],
                                    v_h[:, c * 128:(c + 1) * 128],
                                    pb[:, 512:1024], start=False, stop=True),
                           "mCX")

                    @block.scalar
                    def _(s_, mST0=mST0, mCX0=mCX0):
                        for skc in range(SKC):
                            s_.wait_ge(SEM("mST"), mST0 + skc + 1)
                            if skc >= 2:
                                s_.wait_ge(SEM("mCX"), mCX0 + skc - 1)
                            pp = (PS_A, PS_D)[skc % 2]
                            ti(s_.activation(
                                out=scr2k[:, (skc % 2) * T:(skc % 2 + 1) * T],
                                in_=pp[:, 0:T], func=AF.Exp), "eEX")
                    ctx_mm = cur("mCX")

                    @block.scalar
                    def _(s_, ctx_mm=ctx_mm):
                        s_.wait_ge(SEM("mCX"), ctx_mm)
                        ti(s_.activation(out=st_row[0:1, 0:T],
                                         in_=PS_C[0:1, 0:T], func=AF.Ln),
                           "eAT")
                    ln_s = cur("eAT")

                    @block.tensor
                    def _(t, ln_s=ln_s):
                        t.wait_ge(SEM("eAT"), ln_s)
                        t.matmul(PS_C[:, 0:512], ones_f[0:1, 0:128],
                                 st_row[0:1, 0:512], start=True, stop=True)
                        ti(t.matmul(PS_C[:, 512:1024], ones_f[0:1, 0:128],
                                    st_row[0:1, 512:1024], start=True,
                                    stop=True), "mAT")
                    lnb_t = cur("mAT")

                    @block.scalar
                    def _(s_, lnb_t=lnb_t):
                        s_.wait_ge(SEM("mAT"), lnb_t)
                        ti(s_.activation(out=u32[:], in_=PS_C[:, 0:T],
                                         func=AF.Exp, scale=-1.0), "eAT")
                    prev_inv = cur("eAT")

                    @block.vector
                    def _(v, h=h, inv_s=prev_inv, ctx_mm=ctx_mm):
                        v.wait_ge(SEM("eAT"), inv_s)
                        v.wait_ge(SEM("mCX"), ctx_mm)
                        ti(v.tensor_tensor(out=ctx[:, h * T:(h + 1) * T],
                                           in0=PS_B[:, 0:T], in1=u32[:],
                                           op=OP.mult), "vAT")
                    head_done.append(cur("vAT"))

                sync_all()
                free("k_h", "v_h", "q_h", "scr2k", "kslb0", "kslb1",
                     "vslb0", "vslb1", "qslb0", "qslb1", "ckv", "cq",
                     "wqr_bf", "wkr_bf", "cosq_sb", "sinq_sb", "cosk_sb",
                     "sink_sb", "buvb_sb")

                wob = alloc("wob", [128, DC * T], BF16)
                for half in range(2):
                    @block.gpsimd
                    def _(g, half=half, tgt=cur("mAT")):
                        if half == 1:
                            g.wait_ge(SEM("mAT"), tgt)
                        ti(g.dma_start(
                            out=wob[:].rearrange("p (c o) -> p c o", o=T),
                            in_=wo[:, half * T:(half + 1) * T].rearrange(
                                "(c p) o -> p c o", p=128)), "dAW", 16)
                    wo_d = cur("dAW")
                    wo_ev0 = cur("vAT")

                    @block.tensor
                    def _(t, half=half, wo_d=wo_d, wo_ev0=wo_ev0):
                        t.wait_ge(SEM("dAW"), wo_d)
                        for dc in range(8):
                            pp = (PS_A, PS_B)[dc % 2]
                            if dc >= 2:
                                t.wait_ge(SEM("vAT"), wo_ev0 + dc - 1)
                            for hc in range(H):
                                t.matmul(pp[:, 0:512],
                                         wob[:, hc * T + dc * 128:
                                             hc * T + (dc + 1) * 128],
                                         ctx[:, hc * T:hc * T + 512],
                                         start=(hc == 0), stop=(hc == H - 1))
                                i2 = t.matmul(
                                    pp[:, 512:1024],
                                    wob[:, hc * T + dc * 128:
                                        hc * T + (dc + 1) * 128],
                                    ctx[:, hc * T + 512:(hc + 1) * T],
                                    start=(hc == 0), stop=(hc == H - 1))
                            ti(i2, "mAT")
                    wo_mm0 = cur("mAT") - 8

                    @block.vector
                    def _(v, half=half, wo_mm0=wo_mm0):
                        for dc in range(8):
                            v.wait_ge(SEM("mAT"), wo_mm0 + dc + 1)
                            col = half * 8 + dc
                            pp = (PS_A, PS_B)[dc % 2]
                            ti(v.scalar_tensor_tensor(
                                out=x_sb[:, col * T:(col + 1) * T],
                                in0=pp[:, 0:T],
                                scalar=bo_sb[:, col:col + 1],
                                in1=x_sb[:, col * T:(col + 1) * T],
                                op0=OP.add, op1=OP.add), "vAT")
                sync_all()
                free("wob")

            mpos = alloc("mpos", [128, T])
            mneg = alloc("mneg", [128, T])
            nx_v = ln_phase(act16, True, mpos, mneg)
            sync_all()
            free("x_sb")
            h_sb = alloc("h_sb", [128, OCH * T], FP16)
            tstg = [alloc(f"tstg{i}", [128, 4096], BF16) for i in range(2)]
            ffn(True, act16, h_sb, mpos, mneg, tstg, nx_v)
            free("h_sb", "mpos", "mneg", "tstg0", "tstg1", "babs", "wstg0",
                 "wstg1")

            if upto == 1:
                @block.sync
                def _(sy):
                    for c in range(DC):
                        ti(sy.dma_start(
                            out=out_fm[c * 128:(c + 1) * 128, :],
                            in_=xres_d[:, c * T:(c + 1) * T]), "dXO", 16)
                sync_all()
            if upto >= 2:
                attention()
            if upto == 2:
                @block.sync
                def _(sy):
                    for c in range(DC):
                        ti(sy.dma_start(
                            out=out_fm[c * 128:(c + 1) * 128, :],
                            in_=x_sb[:, c * T:(c + 1) * T]), "dXO", 16)
                sync_all()
            if upto >= 3:
                mpos = alloc("mpos2", [128, T])
                mneg = alloc("mneg2", [128, T])
                babs = alloc("babs2", [128, 4096], BF16)
                nx_v = ln_phase(act16, True, mpos, mneg)

                @block.sync
                def _(sy):
                    ti(sy.dma_start(out=xres_d, in_=x_sb[:]), "dXO", 16)
                sync_all()
                free("x_sb2")
                h_sb = alloc("h_sb2", [128, OCH * T], FP16)
                tstg = [alloc(f"tstg{i}b", [128, 4096], BF16)
                        for i in range(2)]
                ffn(False, act16, h_sb, mpos, mneg, tstg, nx_v)
                free("h_sb2", "mpos2", "mneg2", "babs2", "tstg0b", "tstg1b")

    return nc


def _stage_inputs(inputs):
    f32 = lambda a: np.ascontiguousarray(np.asarray(a, dtype=np.float32))
    x = f32(inputs["L0_attn_hidden_states"])
    wupT = f32(np.asarray(inputs["L0_ffn_w_up"]).T)
    wdownT = f32(np.asarray(inputs["L0_ffn_w_down"]).T)
    col = lambda v, c: f32(np.asarray(v, dtype=np.float32).reshape(c, 128).T)
    theta = 10000.0
    inv_freq = 1.0 / theta ** (np.arange(0, RD, 2, dtype=np.float32) / RD)
    tpos = np.arange(S, dtype=np.float32)
    fr = tpos[:, None] * inv_freq[None, :]
    cosk = f32(np.cos(fr).T)
    sink = f32(np.sin(fr).T)
    ident = np.eye(128, dtype=np.float32)
    buvb = f32(np.broadcast_to(np.asarray(inputs["L0_attn_b_UV"])[None, :],
                               (128, DM)))
    shared = dict(
        wdkv=f32(inputs["L0_attn_W_DKV"]), wuk=f32(inputs["L0_attn_W_UK"]),
        wuv=f32(inputs["L0_attn_W_UV"]), wdq=f32(inputs["L0_attn_W_DQ"]),
        wuq=f32(inputs["L0_attn_W_UQ"]), wqr=f32(inputs["L0_attn_W_QR"]),
        wkr=f32(inputs["L0_attn_W_KR"]), wo=f32(inputs["L0_attn_W_O"]),
        wupT=wupT, wdownT=wdownT,
        b_dkv=col(inputs["L0_attn_b_DKV"], KVCC),
        b_uk=col(inputs["L0_attn_b_UK"], DC),
        b_dq=col(inputs["L0_attn_b_DQ"], QCC),
        b_uq=col(inputs["L0_attn_b_UQ"], DC),
        b_o=col(inputs["L0_attn_b_O"], DC), buv_b=buvb,
        cosk=cosk, sink=sink, ident_in=ident,
    )
    in_maps = []
    for c in range(8):
        seq, half = c // 2, c % 2
        xs = f32(x[seq, half * T:(half + 1) * T, :].T)
        m = dict(shared)
        m["x_fm"] = xs
        m["cosq"] = f32(cosk[:, half * T:(half + 1) * T])
        m["sinq"] = f32(sink[:, half * T:(half + 1) * T])
        in_maps.append(m)
    return in_maps


_CACHE = {}


def kernel(upto=3, **inputs):
    if upto not in _CACHE:
        _CACHE[upto] = build_nc(upto)
    nc = _CACHE[upto]
    in_maps = _stage_inputs(inputs)
    res = run_bass_kernel_spmd(nc, in_maps, core_ids=list(range(8)))
    out = np.empty((B, S, DM), dtype=np.float32)
    for c in range(8):
        seq, half = c // 2, c % 2
        out[seq, half * T:(half + 1) * T, :] = res.results[c]["out_fm"].T
    return out


# version 03 (final)
# speedup vs baseline: 1.0132x; 1.0132x over previous
from contextlib import ExitStack

import numpy as np

import concourse.bass as bass
import concourse.mybir as mybir
from concourse.bass_utils import run_bass_kernel_spmd

F32 = mybir.dt.float32
BF16 = mybir.dt.bfloat16
FP16 = mybir.dt.float16
AX = mybir.AxisListType.X
OP = mybir.AluOpType
AF = mybir.ActivationFunctionType

B, S, DM = 4, 2048, 2048
T = 1024
H, HD = 16, 128
KVC, QC, RD = 512, 1536, 64
IF = 4096
DC = DM // 128
OCH = IF // 128
QCC = QC // 128
KVCC = KVC // 128
SKC = S // 128
ISD = 1.0 / float(np.sqrt(HD))
RB = 1536.0
NUP, NDN = OCH, DC


def build_nc(upto=3):
    nc = bass.Bass("TRN2", target_bir_lowering=False, debug=False,
                   detect_race_conditions=False)

    di = lambda n, s: nc.dram_tensor(n, s, F32, kind="ExternalInput").ap()
    x_in = di("x_fm", [DM, T])
    wdkv, wuk, wuv = di("wdkv", [DM, KVC]), di("wuk", [KVC, DM]), di("wuv", [KVC, DM])
    wdq, wuq = di("wdq", [DM, QC]), di("wuq", [QC, DM])
    wqr, wkr = di("wqr", [HD, RD]), di("wkr", [HD, RD])
    wo = di("wo", [DM, DM])
    wupT, wdownT = di("wupT", [DM, IF]), di("wdownT", [IF, DM])
    b_dkv, b_uk = di("b_dkv", [128, KVCC]), di("b_uk", [128, DC])
    b_dq, b_uq, b_o = di("b_dq", [128, QCC]), di("b_uq", [128, DC]), di("b_o", [128, DC])
    buv_b = di("buv_b", [128, DM])
    cosq, sinq = di("cosq", [RD // 2, T]), di("sinq", [RD // 2, T])
    cosk, sink = di("cosk", [RD // 2, S]), di("sink", [RD // 2, S])
    ident_in = di("ident_in", [128, 128])
    out_fm = nc.dram_tensor("out_fm", [DM, T], F32, kind="ExternalOutput").ap()
    tern_up_d = nc.dram_tensor("tern_up_d", [NUP, 128, DC * 128], BF16).ap()
    tern_dn_d = nc.dram_tensor("tern_dn_d", [NDN, 128, OCH * 128], BF16).ap()
    xres_d = nc.dram_tensor("xres_d", [128, DC * T], F32).ap()
    cc_in = nc.dram_tensor("cc_in", [KVCC * 128 * T], BF16).ap()
    cc_out = nc.dram_tensor("cc_out", [2, KVCC * 128 * T], BF16).ap()

    es = ExitStack()
    sb = lambda n, s, d=F32: es.enter_context(nc.sbuf_tensor(n, s, d))

    bc_r = sb("bc_r", [128, T])
    bc_b = sb("bc_b", [128, T])
    bc_f = sb("bc_f", [128, T])
    u32 = sb("u32", [128, T])
    uh16 = sb("uh16", [128, T], FP16)
    sq16 = [sb(f"sq16_{i}", [128, T], BF16) for i in range(2)]
    ident = sb("ident", [128, 128])
    ones_bf = sb("ones_bf", [128, 128], BF16)
    ones_f = sb("ones_f", [128, 128])
    st_tm = sb("st_tm", [128, 32])
    st_row = sb("st_row", [1, T])
    srow16 = sb("srow16", [1, T], BF16)
    sc = sb("sc", [128, 8])
    padt = sb("padt", [128, 8])
    epsln = sb("epsln", [128, 1])
    accw = sb("accw", [128, 64])
    tiny = sb("tiny", [1, 16])
    bdkv_sb = sb("bdkv_sb", [128, KVCC])
    buk_sb = sb("buk_sb", [128, DC])
    bdq_sb = sb("bdq_sb", [128, QCC])
    buq_sb = sb("buq_sb", [128, DC])
    bo_sb = sb("bo_sb", [128, DC])

    PS_A = es.enter_context(nc.psum_tensor("PS_A", [128, T], F32))
    PS_B = es.enter_context(nc.psum_tensor("PS_B", [128, T], F32))
    PS_C = es.enter_context(nc.psum_tensor("PS_C", [128, T], F32))
    PS_D = es.enter_context(nc.psum_tensor("PS_D", [128, T], F32))

    sems = {}
    vals = {}
    for n in ("dCN", "dWU", "dWD", "dTU", "dTD", "dXC", "dXO", "dAW", "dCC",
              "dRW",
              "aW", "sqP", "qUP", "mUP", "eUP", "hUP", "qDN", "mDN", "eDN",
              "mST", "eEX", "mCX", "mAT", "eAT", "vAT", "cCC",
              "gT", "gV", "gS", "gG"):
        sems[n] = es.enter_context(nc.semaphore(n))
        vals[n] = 0
    SEM = lambda n: sems[n]

    def ti(inst, n, k=1):
        vals[n] += k
        inst.then_inc(sems[n], k)
        return vals[n]

    cur = lambda n: vals[n]

    managed = {}
    _seq = [0]

    def alloc(name, shape, dtype=F32, side="left"):
        cm = nc.sbuf_tensor(name, shape, dtype, side=side)
        h = cm.__enter__()
        _seq[0] += 1
        managed[name] = (_seq[0], cm)
        return h

    def free(*names):
        for n in sorted(names, key=lambda n: -managed[n][0]):
            managed.pop(n)[1].__exit__(None, None, None)

    with es:
        with nc.Block() as block:
            def sync_all(*extra):
                snap = [(n, vals[n]) for n in sems if vals[n] > 0]
                for eng in ("tensor", "vector", "scalar", "gpsimd", "sync"):
                    def _w(e):
                        for n, v in snap:
                            e.wait_ge(sems[n], v)
                    getattr(block, eng)(_w)

            def vpad(v, n=1):
                for _ in range(n):
                    v.tensor_scalar(out=padt[:], in0=padt[:], scalar1=1.0,
                                    scalar2=None, op0=OP.mult)

            def mark(eng, name):
                def _f(e):
                    e.sem_inc(sems[name], 1)
                getattr(block, eng)(_f)
                vals[name] += 1
                return vals[name]

            def bcast_tm(cols8, out_tile):
                v0 = cur("gV")

                @block.tensor
                def _(t):
                    t.wait_ge(SEM("gV"), v0)
                    for j, col in enumerate(cols8):
                        i = t.transpose(PS_D[0:1, j * 128:(j + 1) * 128],
                                        st_tm[:, col:col + 1], ident[:])
                    ti(i, "gT")
                tp = cur("gT")

                @block.vector
                def _(v):
                    v.wait_ge(SEM("gT"), tp)
                    ti(v.tensor_copy(out=st_row[0:1, 0:T], in_=PS_D[0:1, 0:T]),
                       "gV")
                rw = cur("gV")

                @block.tensor
                def _(t):
                    t.wait_ge(SEM("gV"), rw)
                    t.matmul(PS_C[:, 0:512], ones_f[0:1, 0:128],
                             st_row[0:1, 0:512], start=True, stop=True)
                    ti(t.matmul(PS_C[:, 512:1024], ones_f[0:1, 0:128],
                                st_row[0:1, 512:1024], start=True, stop=True),
                       "gT")
                bm = cur("gT")

                @block.vector
                def _(v):
                    v.wait_ge(SEM("gT"), bm)
                    ti(v.tensor_copy(out=out_tile[:], in_=PS_C[:, 0:T]), "gV")

            act16 = alloc("act16", [128, DC * T], BF16, side="right")
            x_sb = alloc("x_sb", [128, DC * T], side="right")

            @block.sync
            def _(sy):
                for dst, src in [
                    (x_sb[:].rearrange("p (c t) -> p c t", c=DC),
                     x_in.rearrange("(c p) t -> p c t", p=128)),
                    (ident[:], ident_in),
                    (bdkv_sb[:], b_dkv),
                    (buk_sb[:], b_uk),
                    (bdq_sb[:], b_dq),
                    (buq_sb[:], b_uq),
                    (bo_sb[:], b_o),
                ]:
                    ti(sy.dma_start(out=dst, in_=src), "dCN", 16)
            consts_dma = cur("dCN")

            @block.vector
            def _(v):
                v.memset(ones_bf[:], 1.0)
                v.memset(ones_f[:], 1.0)
                v.memset(accw[:], 0.0)
                v.memset(st_tm[:], 0.0)
                v.memset(st_row[:], 0.0)
                v.memset(padt[:], 0.0)
                v.memset(epsln[:], 1e-5)
                v.wait_ge(SEM("dCN"), consts_dma)
                ti(v.tensor_scalar(out=buq_sb[:], in0=buq_sb[:], scalar1=ISD,
                                   scalar2=None, op0=OP.mult), "gV")
            consts_v = cur("gV")

            wstg = [alloc(f"wstg{i}", [128, 4096]) for i in range(2)]
            babs = alloc("babs", [128, 4096], BF16)

            @block.gpsimd
            def _(g):
                for i in range(NUP):
                    if i >= 2:
                        g.wait_ge(SEM("aW"), i - 1)
                    ti(g.dma_start(
                        out=wstg[i % 2][:, 0:2048].rearrange(
                            "p (c o) -> p c o", o=128),
                        in_=wupT[:, i * 128:(i + 1) * 128].rearrange(
                            "(c p) o -> p c o", p=128)), "dWU", 16)
                for i in range(NDN):
                    g.wait_ge(SEM("aW"), NUP + i - 1 if i >= 2 else NUP - 1 + i)
                    ti(g.dma_start(
                        out=wstg[i % 2][:, 0:4096].rearrange(
                            "p (c o) -> p c o", o=128),
                        in_=wdownT[:, i * 128:(i + 1) * 128].rearrange(
                            "(c p) o -> p c o", p=128)), "dWU", 16)

            @block.scalar
            def _(s_):
                for i in range(NUP):
                    s_.wait_ge(SEM("dWU"), (i + 1) * 16)
                    ti(s_.activation(out=babs[:, 0:2048],
                                     in_=wstg[i % 2][:, 0:2048], func=AF.Abs,
                                     accum_out=accw[:, i:i + 1]), "aW")
                for i in range(NDN):
                    s_.wait_ge(SEM("dWU"), (NUP + i + 1) * 16)
                    s_.activation(out=babs[:, 0:2048],
                                  in_=wstg[i % 2][:, 0:2048], func=AF.Abs,
                                  accum_out=accw[:, 32 + 2 * i:33 + 2 * i])
                    ti(s_.activation(out=babs[:, 0:2048],
                                     in_=wstg[i % 2][:, 2048:4096], func=AF.Abs,
                                     accum_out=accw[:, 33 + 2 * i:34 + 2 * i]),
                       "aW")
            abs_done = cur("aW")

            @block.vector
            def _(v):
                v.wait_ge(SEM("aW"), abs_done)
                v.reduce_sum(out=st_tm[:, 16:17], in_=accw[:, 0:32], axis=AX)
                ti(v.reduce_sum(out=st_tm[:, 17:18], in_=accw[:, 32:64],
                                axis=AX), "gV")
            v1 = cur("gV")

            @block.tensor
            def _(t):
                t.wait_ge(SEM("gV"), v1)
                ti(t.matmul(PS_C[0:1, 0:2], ones_f[:, 0:1], st_tm[:, 16:18],
                            start=True, stop=True), "gT")
            t1 = cur("gT")

            @block.vector
            def _(v):
                v.wait_ge(SEM("gT"), t1)
                v.tensor_scalar(out=tiny[0:1, 0:2], in0=PS_C[0:1, 0:2],
                                scalar1=1.0 / (DM * IF), scalar2=1e-5,
                                op0=OP.mult, op1=OP.max)
                vpad(v)
                ti(v.tensor_copy(out=st_row[0:1, 0:2], in_=tiny[0:1, 0:2]),
                   "gV")
            v2 = cur("gV")

            @block.tensor
            def _(t):
                t.wait_ge(SEM("gV"), v2)
                ti(t.matmul(PS_C[:, 0:2], ones_f[0:1, 0:128],
                            st_row[0:1, 0:2], start=True, stop=True), "gT")
            t2 = cur("gT")

            @block.vector
            def _(v):
                v.wait_ge(SEM("gT"), t2)
                v.tensor_copy(out=sc[:, 4:6], in_=PS_C[:, 0:2])
                vpad(v)
                v.tensor_scalar(out=sc[:, 0:1], in0=sc[:, 4:5], scalar1=0.5,
                                scalar2=None, op0=OP.mult)
                v.tensor_scalar(out=sc[:, 1:2], in0=sc[:, 4:5], scalar1=-0.5,
                                scalar2=None, op0=OP.mult)
                v.tensor_scalar(out=sc[:, 2:3], in0=sc[:, 5:6], scalar1=0.5,
                                scalar2=None, op0=OP.mult)
                ti(v.tensor_scalar(out=sc[:, 3:4], in0=sc[:, 5:6],
                                   scalar1=-0.5, scalar2=None, op0=OP.mult),
                   "gV")
            sc_v = cur("gV")

            def ln_phase(lnx, quant, mpos, mneg):
                sq0 = cur("sqP")
                st0 = cur("gT")

                @block.scalar
                def _(s_):
                    s_.wait_ge(SEM("gV"), cur("gV"))
                    for c in range(DC):
                        if c >= 2:
                            s_.wait_ge(SEM("gT"), st0 + c - 1)
                        ti(s_.activation(out=sq16[c % 2][:],
                                         in_=x_sb[:, c * T:(c + 1) * T],
                                         func=AF.Square), "sqP")

                @block.tensor
                def _(t, gv0=None):
                    t.wait_ge(SEM("gV"), cur("gV"))
                    t.wait_ge(SEM("gS"), cur("gS"))
                    t.wait_ge(SEM("eAT"), cur("eAT"))
                    t.wait_ge(SEM("vAT"), cur("vAT"))
                    for c in range(DC):
                        t.matmul(PS_C[:, 0:512], ones_f[:, 0:128],
                                 x_sb[:, c * T:c * T + 512],
                                 start=(c == 0), stop=(c == DC - 1))
                        t.matmul(PS_C[:, 512:1024], ones_f[:, 0:128],
                                 x_sb[:, c * T + 512:(c + 1) * T],
                                 start=(c == 0), stop=(c == DC - 1))
                        t.wait_ge(SEM("sqP"), sq0 + c + 1)
                        t.matmul(PS_D[:, 0:512], ones_bf[:, 0:128],
                                 sq16[c % 2][:, 0:512],
                                 start=(c == 0), stop=(c == DC - 1))
                        ti(t.matmul(PS_D[:, 512:1024], ones_bf[:, 0:128],
                                    sq16[c % 2][:, 512:1024],
                                    start=(c == 0), stop=(c == DC - 1)), "gT")
                stats_t = cur("gT")

                @block.vector
                def _(v):
                    v.wait_ge(SEM("gT"), stats_t)
                    v.tensor_scalar(out=bc_b[:], in0=PS_C[:, 0:T],
                                    scalar1=1.0 / DM, scalar2=None,
                                    op0=OP.mult)
                    v.tensor_tensor(out=u32[:], in0=bc_b[:], in1=bc_b[:],
                                    op=OP.mult)
                    ti(v.scalar_tensor_tensor(out=u32[:], in0=PS_D[:, 0:T],
                                              scalar=1.0 / DM, in1=u32[:],
                                              op0=OP.mult, op1=OP.subtract),
                       "gV")
                var_v = cur("gV")

                @block.scalar
                def _(s_):
                    s_.wait_ge(SEM("gV"), var_v)
                    s_.activation(out=bc_f[:], in_=u32[:], func=AF.Ln,
                                  bias=epsln[:, 0:1])
                    ti(s_.activation(out=bc_r[:], in_=bc_f[:], func=AF.Exp,
                                     scale=-0.5), "gS")
                r_s = cur("gS")

                @block.vector
                def _(v):
                    v.wait_ge(SEM("gS"), r_s)
                    v.tensor_tensor(out=bc_b[:], in0=bc_b[:], in1=bc_r[:],
                                    op=OP.mult)
                    for c in range(DC):
                        v.tensor_tensor(out=u32[:],
                                        in0=x_sb[:, c * T:(c + 1) * T],
                                        in1=bc_r[:], op=OP.mult)
                        if not quant:
                            i = v.tensor_tensor(out=lnx[:, c * T:(c + 1) * T],
                                                in0=u32[:], in1=bc_b[:],
                                                op=OP.subtract)
                        else:
                            v.tensor_tensor(out=u32[:], in0=u32[:],
                                            in1=bc_b[:], op=OP.subtract)
                            if c == 0:
                                v.tensor_copy(out=mpos[:, 0:T], in_=u32[:])
                                i = v.tensor_copy(out=mneg[:, 0:T], in_=u32[:])
                            else:
                                v.tensor_tensor(out=mpos[:, 0:T],
                                                in0=mpos[:, 0:T],
                                                in1=u32[:], op=OP.max)
                                i = v.tensor_tensor(
                                    out=mneg[:, 0:T], in0=mneg[:, 0:T],
                                    in1=u32[:], op=OP.min)
                    ti(i, "gV")
                app_v = cur("gV")
                if not quant:
                    return app_v

                @block.vector
                def _(v):
                    v.tensor_scalar(out=u32[:], in0=mneg[:, 0:T], scalar1=-1.0,
                                    scalar2=None, op0=OP.mult)
                    ti(v.tensor_tensor(out=mpos[:, 0:T], in0=mpos[:, 0:T],
                                       in1=u32[:], op=OP.max), "gV")
                am_v = cur("gV")

                @block.tensor
                def _(t):
                    t.wait_ge(SEM("gV"), am_v)
                    for j in range(8):
                        i = t.transpose(PS_D[:, j * 128:(j + 1) * 128],
                                        mpos[:, j * 128:(j + 1) * 128],
                                        ident[:])
                    ti(i, "gT")
                tp_t = cur("gT")

                @block.vector
                def _(v):
                    v.wait_ge(SEM("gT"), tp_t)
                    for j in range(8):
                        v.tensor_reduce(out=st_tm[:, j:j + 1],
                                        in_=PS_D[:, j * 128:(j + 1) * 128],
                                        axis=AX, op=OP.max)
                    vpad(v)
                    v.tensor_scalar(out=st_tm[:, 8:16], in0=st_tm[:, 0:8],
                                    scalar1=1e-5, scalar2=None, op0=OP.max)
                    vpad(v)
                    v.reciprocal(out=st_tm[:, 0:8], in_=st_tm[:, 8:16])
                    vpad(v)
                    ti(v.tensor_scalar(out=st_tm[:, 0:8], in0=st_tm[:, 0:8],
                                       scalar1=127.0, scalar2=None,
                                       op0=OP.mult), "gV")
                bcast_tm(list(range(8)), bc_f)
                qs_v = cur("gV")

                @block.vector
                def _(v):
                    v.wait_ge(SEM("gV"), qs_v)
                    v.tensor_tensor(out=bc_r[:], in0=bc_r[:], in1=bc_f[:],
                                    op=OP.mult)
                    v.tensor_tensor(out=bc_b[:], in0=bc_b[:], in1=bc_f[:],
                                    op=OP.mult)
                    for c in range(DC):
                        v.tensor_tensor(out=u32[:],
                                        in0=x_sb[:, c * T:(c + 1) * T],
                                        in1=bc_r[:], op=OP.mult)
                        v.tensor_tensor(out=u32[:], in0=u32[:], in1=bc_b[:],
                                        op=OP.subtract)
                        v.tensor_scalar(out=uh16[:], in0=u32[:], scalar1=RB,
                                        scalar2=None, op0=OP.add)
                        v.tensor_scalar(out=u32[:], in0=uh16[:], scalar1=RB,
                                        scalar2=127.0, op0=OP.subtract,
                                        op1=OP.min)
                        ti(v.tensor_scalar(out=lnx[:, c * T:(c + 1) * T],
                                           in0=u32[:], scalar1=-127.0,
                                           scalar2=None, op0=OP.max), "gV")
                return cur("gV")

            def ffn(first, lnx, h_sb, mpos, mneg, tstg, nx_ready):
                tern_src_up = None if first else tern_up_d
                xsrc = x_in if first else xres_d
                final = (not first) and upto >= 3

                qUP0, mUP0, eUP0, hUP0 = (cur("qUP"), cur("mUP"), cur("eUP"),
                                          cur("hUP"))
                dWU0, dTU0 = cur("dWU"), cur("dTU")

                if first:
                    @block.gpsimd
                    def _(g):
                        def tstore(j):
                            g.wait_ge(SEM("qUP"), qUP0 + j + 1)
                            ti(g.dma_start(out=tern_up_d[j],
                                           in_=tstg[j % 2][:, 0:2048]),
                               "dTU", 16)

                        for i in range(NUP):
                            if i >= 2:
                                tstore(i - 2)
                                g.wait_ge(SEM("qUP"), qUP0 + i - 1)
                            else:
                                g.wait_ge(SEM("aW"), NUP + NDN)
                            ti(g.dma_start(
                                out=wstg[i % 2][:, 0:2048].rearrange(
                                    "p (c o) -> p c o", o=128),
                                in_=wupT[:, i * 128:(i + 1) * 128].rearrange(
                                    "(c p) o -> p c o", p=128)), "dWU", 16)
                        tstore(NUP - 2)
                        tstore(NUP - 1)

                    def up_quant(v, i):
                        v.wait_ge(SEM("dWU"), dWU0 + (i + 1) * 16)
                        if i >= 2:
                            v.wait_ge(SEM("mUP"), mUP0 + i - 1)
                            v.wait_ge(SEM("dTU"), dTU0 + (i - 1) * 16)
                        v.tensor_scalar(out=babs[:, 0:2048],
                                        in0=wstg[i % 2][:, 0:2048],
                                        scalar1=sc[:, 1:2], scalar2=None,
                                        op0=OP.is_le)
                        ti(v.scalar_tensor_tensor(
                            out=tstg[i % 2][:, 0:2048],
                            in0=wstg[i % 2][:, 0:2048],
                            scalar=sc[:, 0:1], in1=babs[:, 0:2048],
                            op0=OP.is_ge, op1=OP.subtract), "qUP")

                else:
                    @block.gpsimd
                    def _(g):
                        for i in range(NUP):
                            if i >= 2:
                                g.wait_ge(SEM("mUP"), mUP0 + i - 1)
                            ti(g.dma_start(out=tstg[i % 2][:, 0:2048],
                                           in_=tern_up_d[i]), "dTU", 16)

                nx0 = nx_ready - DC

                @block.tensor
                def _(t):
                    for i in range(NUP):
                        if first:
                            t.wait_ge(SEM("qUP"), qUP0 + i + 1)
                        else:
                            t.wait_ge(SEM("dTU"), dTU0 + (i + 1) * 16)
                        if i >= 2:
                            t.wait_ge(SEM("eUP"), eUP0 + i - 1)
                        pp = (PS_A, PS_B)[i % 2]
                        for c in range(DC):
                            if i < 2:
                                t.wait_ge(SEM("gV"), nx0 + c + 1)
                            t.matmul(pp[:, 0:512],
                                     tstg[i % 2][:, c * 128:(c + 1) * 128],
                                     lnx[:, c * T:c * T + 512],
                                     start=(c == 0), stop=(c == DC - 1))
                            i2 = t.matmul(
                                pp[:, 512:1024],
                                tstg[i % 2][:, c * 128:(c + 1) * 128],
                                lnx[:, c * T + 512:(c + 1) * T],
                                start=(c == 0), stop=(c == DC - 1))
                        ti(i2, "mUP")

                @block.scalar
                def _(s_):
                    for i in range(NUP):
                        s_.wait_ge(SEM("mUP"), mUP0 + i + 1)
                        pp = (PS_A, PS_B)[i % 2]
                        ti(s_.activation(out=h_sb[:, i * T:(i + 1) * T],
                                         in_=pp[:, 0:T],
                                         func=AF.Relu, scale=0.125), "eUP")

                def up_hsq(v, i):
                    v.wait_ge(SEM("eUP"), eUP0 + i + 1)
                    v.tensor_tensor(out=u32[:], in0=h_sb[:, i * T:(i + 1) * T],
                                    in1=h_sb[:, i * T:(i + 1) * T],
                                    op=OP.mult)
                    if i == 0:
                        i2 = v.tensor_copy(out=mpos[:, 0:T], in_=u32[:])
                    else:
                        i2 = v.tensor_tensor(
                            out=mpos[:, 0:T], in0=mpos[:, 0:T],
                            in1=u32[:], op=OP.max)
                    ti(i2, "hUP")

                @block.vector
                def _(v):
                    if first:
                        v.wait_ge(SEM("gV"), sc_v)
                        for i in range(NUP):
                            up_quant(v, i)
                            if i >= 2:
                                up_hsq(v, i - 2)
                        up_hsq(v, NUP - 2)
                        up_hsq(v, NUP - 1)
                    else:
                        for i in range(NUP):
                            up_hsq(v, i)
                hmax_v = cur("hUP")

                @block.tensor
                def _(t):
                    t.wait_ge(SEM("hUP"), hmax_v)
                    for j in range(8):
                        i = t.transpose(PS_D[:, j * 128:(j + 1) * 128],
                                        mpos[:, j * 128:(j + 1) * 128],
                                        ident[:])
                    ti(i, "gT")
                tph = cur("gT")

                @block.vector
                def _(v):
                    v.wait_ge(SEM("gT"), tph)
                    for j in range(8):
                        v.tensor_reduce(out=st_tm[:, j:j + 1],
                                        in_=PS_D[:, j * 128:(j + 1) * 128],
                                        axis=AX, op=OP.max)
                    v.tensor_scalar(out=st_tm[:, 24:32], in0=st_tm[:, 8:16],
                                    scalar1=sc[:, 4:5], scalar2=8.0 / 127.0,
                                    op0=OP.mult, op1=OP.mult)
                    vpad(v)
                    v.tensor_tensor(out=st_tm[:, 24:32], in0=st_tm[:, 24:32],
                                    in1=st_tm[:, 24:32], op=OP.mult)
                    vpad(v)
                    v.tensor_tensor(out=st_tm[:, 16:24], in0=st_tm[:, 24:32],
                                    in1=st_tm[:, 0:8], op=OP.mult)
                    vpad(v)
                    v.tensor_scalar(out=st_tm[:, 16:24], in0=st_tm[:, 16:24],
                                    scalar1=1e-5, scalar2=None, op0=OP.max)
                    vpad(v)
                    v.reciprocal(out=st_tm[:, 0:8], in_=st_tm[:, 16:24])
                    vpad(v)
                    v.tensor_scalar(out=st_tm[:, 0:8], in0=st_tm[:, 0:8],
                                    scalar1=127.0, scalar2=None, op0=OP.mult)
                    vpad(v)
                    ti(v.tensor_tensor(out=st_tm[:, 0:8], in0=st_tm[:, 0:8],
                                       in1=st_tm[:, 24:32], op=OP.mult), "gV")
                bcast_tm(list(range(8)), bc_b)

                @block.vector
                def _(v):
                    ti(v.tensor_scalar(out=st_tm[:, 16:24],
                                       in0=st_tm[:, 16:24],
                                       scalar1=sc[:, 5:6], scalar2=0.5 / 127.0,
                                       op0=OP.mult, op1=OP.mult), "gV")
                bcast_tm(list(range(16, 24)), bc_f)
                fb_v = cur("gV")

                qDN0, mDN0, eDN0 = cur("qDN"), cur("mDN"), cur("eDN")
                dWD0, dTD0, dXC0, dXO0 = (cur("dWD"), cur("dTD"), cur("dXC"),
                                          cur("dXO"))

                if first:
                    def dn_quant(v, i):
                        v.wait_ge(SEM("dWD"), dWD0 + (i + 1) * 16)
                        if i >= 2:
                            v.wait_ge(SEM("mDN"), mDN0 + i - 1)
                            v.wait_ge(SEM("dTD"), dTD0 + (i - 1) * 16)
                        else:
                            v.wait_ge(SEM("mUP"), mUP0 + NUP)
                        v.tensor_scalar(out=babs[:, 0:4096],
                                        in0=wstg[i % 2][:, 0:4096],
                                        scalar1=sc[:, 3:4], scalar2=None,
                                        op0=OP.is_le)
                        ti(v.scalar_tensor_tensor(
                            out=tstg[i % 2][:, 0:4096],
                            in0=wstg[i % 2][:, 0:4096],
                            scalar=sc[:, 2:3], in1=babs[:, 0:4096],
                            op0=OP.is_ge, op1=OP.subtract), "qDN")

                    @block.vector
                    def _(v):
                        v.wait_ge(SEM("gV"), sc_v)
                        dn_quant(v, 0)
                        dn_quant(v, 1)

                nh0 = cur("gV")

                @block.vector
                def _(v):
                    v.wait_ge(SEM("gV"), fb_v)
                    for i in range(OCH):
                        v.tensor_tensor(out=u32[:],
                                        in0=h_sb[:, i * T:(i + 1) * T],
                                        in1=h_sb[:, i * T:(i + 1) * T],
                                        op=OP.mult)
                        v.tensor_tensor(out=u32[:], in0=u32[:],
                                        in1=bc_b[:], op=OP.mult)
                        v.tensor_scalar(out=uh16[:], in0=u32[:], scalar1=RB,
                                        scalar2=None, op0=OP.add)
                        ti(v.tensor_scalar(out=h_sb[:, i * T:(i + 1) * T],
                                           in0=uh16[:], scalar1=RB,
                                           scalar2=127.0, op0=OP.subtract,
                                           op1=OP.min), "gV")
                nh_v = cur("gV")

                tag = "a" if first else "b"
                xc = [alloc(f"xc{i}{tag}", [128, T]) for i in range(2)]

                @block.sync
                def _(sy):
                    def xstore(j):
                        sy.wait_ge(SEM("eDN"), eDN0 + 2 * (j + 1))
                        ti(sy.dma_start(
                            out=out_fm[j * 128:(j + 1) * 128, :] if final
                            else xres_d[:, j * T:(j + 1) * T],
                            in_=xc[j % 2][:]), "dXO", 16)

                    for i in range(NDN):
                        if i >= 2:
                            xstore(i - 2)
                            sy.wait_ge(SEM("eDN"), eDN0 + 2 * (i - 1))
                            sy.wait_ge(SEM("dXO"), dXO0 + (i - 1) * 16)
                        ti(sy.dma_start(
                            out=xc[i % 2][:],
                            in_=xsrc[i * 128:(i + 1) * 128, :] if first
                            else xsrc[:, i * T:(i + 1) * T]), "dXC", 16)
                    xstore(NDN - 2)
                    xstore(NDN - 1)

                if first:
                    @block.gpsimd
                    def _(g):
                        def tstored(j):
                            g.wait_ge(SEM("qDN"), qDN0 + j + 1)
                            ti(g.dma_start(out=tern_dn_d[j],
                                           in_=tstg[j % 2][:, 0:4096]),
                               "dTD", 16)

                        for i in range(NDN):
                            if i >= 2:
                                tstored(i - 2)
                                g.wait_ge(SEM("qDN"), qDN0 + i - 1)
                            else:
                                g.wait_ge(SEM("qUP"), qUP0 + NUP)
                            ti(g.dma_start(
                                out=wstg[i % 2][:, 0:4096].rearrange(
                                    "p (c o) -> p c o", o=128),
                                in_=wdownT[:, i * 128:(i + 1) * 128].rearrange(
                                    "(c p) o -> p c o", p=128)), "dWD", 16)
                        tstored(NDN - 2)
                        tstored(NDN - 1)

                else:
                    @block.gpsimd
                    def _(g):
                        for i in range(NDN):
                            if i >= 2:
                                g.wait_ge(SEM("mDN"), mDN0 + i - 1)
                            else:
                                g.wait_ge(SEM("mUP"), mUP0 + NUP)
                            ti(g.dma_start(out=tstg[i % 2][:, 0:4096],
                                           in_=tern_dn_d[i]), "dTD", 16)

                @block.tensor
                def _(t):
                    for i in range(NDN):
                        if first:
                            t.wait_ge(SEM("qDN"), qDN0 + i + 1)
                        else:
                            t.wait_ge(SEM("dTD"), dTD0 + (i + 1) * 16)
                        if i >= 2:
                            t.wait_ge(SEM("eDN"), eDN0 + 2 * (i - 1))
                        pp = (PS_A, PS_B)[i % 2]
                        for c in range(OCH):
                            if i < 2:
                                t.wait_ge(SEM("gV"), nh0 + c + 1)
                            t.matmul(pp[:, 0:512],
                                     tstg[i % 2][:, c * 128:(c + 1) * 128],
                                     h_sb[:, c * T:c * T + 512],
                                     start=(c == 0), stop=(c == OCH - 1))
                            i2 = t.matmul(
                                pp[:, 512:1024],
                                tstg[i % 2][:, c * 128:(c + 1) * 128],
                                h_sb[:, c * T + 512:(c + 1) * T],
                                start=(c == 0), stop=(c == OCH - 1))
                        ti(i2, "mDN")

                def dn_evac(v, i):
                    v.wait_ge(SEM("mDN"), mDN0 + i + 1)
                    v.wait_ge(SEM("dXC"), dXC0 + (i + 1) * 16)
                    pp = (PS_A, PS_B)[i % 2]
                    ti(v.tensor_tensor(out=u32[:], in0=pp[:, 0:T],
                                       in1=bc_f[:], op=OP.mult), "eDN")
                    ti(v.tensor_tensor(out=xc[i % 2][:], in0=u32[:],
                                       in1=xc[i % 2][:], op=OP.add), "eDN")

                @block.vector
                def _(v):
                    if first:
                        for i in range(2, NDN):
                            dn_quant(v, i)
                            dn_evac(v, i - 2)
                        dn_evac(v, NDN - 2)
                        dn_evac(v, NDN - 1)
                    else:
                        for i in range(NDN):
                            dn_evac(v, i)

                xo_dma = cur("dXO")
                sync_all()
                free(f"xc0{tag}", f"xc1{tag}")
                return xo_dma

            def attention():
                nonlocal x_sb
                x_sb = alloc("x_sb2", [128, DC * T], side="right")

                @block.sync
                def _(sy):
                    ti(sy.dma_start(out=x_sb[:], in_=xres_d), "dCN", 16)
                x1_dma = cur("dCN")

                @block.vector
                def _(v):
                    v.wait_ge(SEM("dCN"), x1_dma)
                    v.sem_inc(SEM("gV"), 1)
                vals["gV"] += 1

                wqr_bf = alloc("wqr_bf", [128, RD], BF16)
                wkr_bf = alloc("wkr_bf", [128, RD], BF16)
                cq = alloc("cq", [128, QCC * T], BF16)
                ckv = alloc("ckv", [128, KVCC * S], BF16)
                y = act16
                y_v = ln_phase(y, False, None, None)

                wdkv_bf = alloc("wdkv_bf", [128, DC * KVC], BF16)
                wdqb = [alloc(f"wdqb{i}", [128, DC * 128], BF16)
                        for i in range(2)]

                @block.gpsimd
                def _(g):
                    ti(g.dma_start(
                        out=wdkv_bf[:].rearrange("p (c k) -> p c k", k=KVC),
                        in_=wdkv.rearrange("(c p) k -> p c k", p=128)),
                       "dAW", 16)
                    ti(g.dma_start(out=wqr_bf[:], in_=wqr), "dAW", 16)
                    ti(g.dma_start(out=wkr_bf[:], in_=wkr), "dAW", 16)
                dkv_w = cur("dAW")

                ckv_st = alloc("ckv_st", [128, KVCC * T], BF16)

                dkv_ev0 = cur("eAT")
                for kc in range(KVCC):
                    @block.tensor
                    def _(t, kc=kc):
                        if kc == 0:
                            t.wait_ge(SEM("gV"), y_v)
                            t.wait_ge(SEM("dAW"), dkv_w)
                        else:
                            t.wait_ge(SEM("eAT"), dkv_ev0 + kc)
                        for ic in range(DC):
                            t.matmul(PS_A[:, 0:512],
                                     wdkv_bf[:, ic * KVC + kc * 128:
                                             ic * KVC + (kc + 1) * 128],
                                     y[:, ic * T:ic * T + 512],
                                     start=(ic == 0), stop=(ic == DC - 1))
                            i2 = t.matmul(PS_A[:, 512:1024],
                                          wdkv_bf[:, ic * KVC + kc * 128:
                                                  ic * KVC + (kc + 1) * 128],
                                          y[:, ic * T + 512:(ic + 1) * T],
                                          start=(ic == 0), stop=(ic == DC - 1))
                        ti(i2, "mAT")

                    @block.scalar
                    def _(s_, kc=kc, tgt=cur("mAT")):
                        s_.wait_ge(SEM("mAT"), tgt)
                        ti(s_.activation(out=ckv_st[:, kc * T:(kc + 1) * T],
                                         in_=PS_A[:, 0:T], func=AF.Identity,
                                         bias=bdkv_sb[:, kc:kc + 1]), "eAT")
                ckv_s = cur("eAT")

                @block.gpsimd
                def _(g):
                    g.wait_ge(SEM("eAT"), ckv_s)
                    ti(g.dma_start(
                        out=cc_in.rearrange("(p f) -> p f", p=128),
                        in_=ckv_st[:]), "dCC", 16)
                    g.wait_ge(SEM("dCC"), cur("dCC"))
                    ti(g.collective_compute(
                        "AllGather", OP.bypass,
                        replica_groups=[[0, 1], [2, 3], [4, 5], [6, 7]],
                        ins=[cc_in], outs=[cc_out.rearrange("a b -> (a b)")]),
                       "cCC", 1)
                cc_done = cur("cCC")

                @block.sync
                def _(sy):
                    sy.wait_ge(SEM("cCC"), cc_done)
                    for half in range(2):
                        src = cc_out[half].rearrange("(p f) -> p f", p=128)
                        for kc in range(KVCC):
                            ti(sy.dma_start(
                                out=ckv[:, kc * S + half * T:
                                        kc * S + (half + 1) * T],
                                in_=src[:, kc * T:(kc + 1) * T]), "dCN", 16)
                ckv_ready = cur("dCN")

                mDQ0 = cur("mAT")
                dAW_dq0 = cur("dAW")

                @block.gpsimd
                def _(g):
                    for i in range(QCC):
                        if i >= 2:
                            g.wait_ge(SEM("mAT"), mDQ0 + i - 1)
                        ti(g.dma_start(
                            out=wdqb[i % 2][:].rearrange(
                                "p (c o) -> p c o", o=128),
                            in_=wdq[:, i * 128:(i + 1) * 128].rearrange(
                                "(c p) o -> p c o", p=128)), "dAW", 16)

                dq_ev0 = cur("eAT")
                for qc in range(QCC):
                    @block.tensor
                    def _(t, qc=qc):
                        t.wait_ge(SEM("dAW"), dAW_dq0 + (qc + 1) * 16)
                        if qc >= 1:
                            t.wait_ge(SEM("eAT"), dq_ev0 + qc)
                        for ic in range(DC):
                            t.matmul(PS_B[:, 0:512],
                                     wdqb[qc % 2][:, ic * 128:(ic + 1) * 128],
                                     y[:, ic * T:ic * T + 512],
                                     start=(ic == 0), stop=(ic == DC - 1))
                            i2 = t.matmul(PS_B[:, 512:1024],
                                          wdqb[qc % 2][:, ic * 128:(ic + 1) * 128],
                                          y[:, ic * T + 512:(ic + 1) * T],
                                          start=(ic == 0), stop=(ic == DC - 1))
                        ti(i2, "mAT")

                    @block.scalar
                    def _(s_, qc=qc, tgt=cur("mAT")):
                        s_.wait_ge(SEM("mAT"), tgt)
                        ti(s_.activation(out=cq[:, qc * T:(qc + 1) * T],
                                         in_=PS_B[:, 0:T], func=AF.Identity,
                                         bias=bdq_sb[:, qc:qc + 1]), "eAT")

                sync_all()
                free("wdkv_bf", "wdqb0", "wdqb1", "ckv_st")

                cosq_sb = alloc("cosq_sb", [RD // 2, T], BF16)
                sinq_sb = alloc("sinq_sb", [RD // 2, T], BF16)
                cosk_sb = alloc("cosk_sb", [RD // 2, S], BF16)
                sink_sb = alloc("sink_sb", [RD // 2, S], BF16)
                buvb_sb = alloc("buvb_sb", [128, DM], BF16)

                @block.gpsimd
                def _(g):
                    ti(g.dma_start(out=buvb_sb[:], in_=buv_b), "dAW", 16)
                    ti(g.dma_start(out=cosq_sb[:], in_=cosq), "dAW", 16)
                    ti(g.dma_start(out=sinq_sb[:], in_=sinq), "dAW", 16)
                    ti(g.dma_start(out=cosk_sb[:], in_=cosk), "dAW", 16)
                    ti(g.dma_start(out=sink_sb[:], in_=sink), "dAW", 16)
                tbl_d = cur("dAW")

                @block.vector
                def _(v):
                    v.wait_ge(SEM("dAW"), tbl_d)
                    v.sem_inc(SEM("vAT"), 1)
                vals["vAT"] += 1

                ctx = act16
                k_h = alloc("k_h", [128, S], BF16)
                v_h = alloc("v_h", [128, S], BF16)
                q_h = alloc("q_h", [128, T], BF16)
                scr2k = alloc("scr2k", [128, S], BF16)
                kslb = [alloc(f"kslb{i}", [128, KVCC * 128], BF16)
                        for i in range(2)]
                vslb = [alloc(f"vslb{i}", [128, KVCC * 128], BF16)
                        for i in range(2)]
                qslb = [alloc(f"qslb{i}", [128, QCC * 128], BF16)
                        for i in range(2)]

                head_done = [cur("vAT")]
                slc_used = []
                prev_inv = cur("eAT")

                for h in range(H):
                    @block.gpsimd
                    def _(g, h=h):
                        if h >= 2:
                            g.wait_ge(SEM("mAT"), slc_used[h - 2])
                        for w_, dst in ((wuk, kslb[h % 2]), (wuv, vslb[h % 2]),
                                        (wuq, qslb[h % 2])):
                            ti(g.dma_start(
                                out=dst[:].rearrange("p (c o) -> p c o", o=128),
                                in_=w_[:, h * 128:(h + 1) * 128].rearrange(
                                    "(c p) o -> p c o", p=128)), "dAW", 16)
                    slw = cur("dAW")

                    a_ev0 = cur("eAT")
                    a_mm0 = cur("mAT")
                    pi = prev_inv
                    hd_prev = head_done[h]

                    @block.tensor
                    def _(t, h=h, slw=slw, a_ev0=a_ev0, pi=pi, hd=hd_prev):
                        t.wait_ge(SEM("dAW"), slw)
                        t.wait_ge(SEM("eAT"), pi)
                        t.wait_ge(SEM("vAT"), hd)
                        if h == 0:
                            t.wait_ge(SEM("dCN"), ckv_ready)
                        for q4 in range(4):
                            if q4 >= 2:
                                t.wait_ge(SEM("eAT"), a_ev0 + q4 - 1)
                            reg = PS_A[:, (q4 % 2) * 512:(q4 % 2) * 512 + 512]
                            for kc in range(KVCC):
                                i2 = t.matmul(
                                    reg,
                                    kslb[h % 2][:, kc * 128:(kc + 1) * 128],
                                    ckv[:, kc * S + q4 * 512:
                                        kc * S + (q4 + 1) * 512],
                                    start=(kc == 0), stop=(kc == KVCC - 1))
                            ti(i2, "mAT")

                    @block.scalar
                    def _(s_, h=h, a_mm0=a_mm0):
                        for q4 in range(4):
                            s_.wait_ge(SEM("mAT"), a_mm0 + q4 + 1)
                            ti(s_.activation(
                                out=k_h[:, q4 * 512:(q4 + 1) * 512],
                                in_=PS_A[:, (q4 % 2) * 512:(q4 % 2) * 512 + 512],
                                func=AF.Identity, bias=buk_sb[:, h:h + 1]), "eAT")
                    k_done = cur("eAT")

                    v_mm0 = cur("mAT")
                    v_ev0 = cur("vAT")

                    @block.tensor
                    def _(t, h=h, k_done=k_done, v_ev0=v_ev0):
                        t.wait_ge(SEM("eAT"), k_done)
                        banks = [PS_B[:, 0:512], PS_B[:, 512:1024],
                                 PS_D[:, 0:512], PS_D[:, 512:1024]]
                        for skc in range(SKC):
                            if skc >= 4:
                                t.wait_ge(SEM("vAT"), v_ev0 + skc - 3)
                            reg = banks[skc % 4][:, 0:128]
                            for kc in range(KVCC):
                                i2 = t.matmul(
                                    reg,
                                    ckv[:, kc * S + skc * 128:
                                        kc * S + (skc + 1) * 128],
                                    vslb[h % 2][:, kc * 128:(kc + 1) * 128],
                                    start=(kc == 0), stop=(kc == KVCC - 1))
                            ti(i2, "mAT")

                    @block.vector
                    def _(v, h=h, v_mm0=v_mm0):
                        banks = [PS_B[:, 0:512], PS_B[:, 512:1024],
                                 PS_D[:, 0:512], PS_D[:, 512:1024]]
                        for skc in range(SKC):
                            v.wait_ge(SEM("mAT"), v_mm0 + skc + 1)
                            ti(v.tensor_tensor(
                                out=v_h[:, skc * 128:(skc + 1) * 128],
                                in0=banks[skc % 4][:, 0:128],
                                in1=buvb_sb[:, h * 128:(h + 1) * 128],
                                op=OP.add), "vAT")
                    v_done = cur("vAT")

                    @block.tensor
                    def _(t, k_done=k_done, v_done=v_done):
                        t.wait_ge(SEM("eAT"), k_done)
                        t.wait_ge(SEM("vAT"), v_done)
                        t.matmul(PS_C[0:RD, 0:512], wkr_bf[:, 0:RD],
                                 k_h[:, 0:512], start=True, stop=True)
                        ti(t.matmul(PS_C[0:RD, 512:1024], wkr_bf[:, 0:RD],
                                    k_h[:, 512:1024], start=True, stop=True),
                           "mAT")
                    kr0 = cur("mAT")

                    @block.vector
                    def _(v, kr0=kr0):
                        v.wait_ge(SEM("mAT"), kr0)
                        v.tensor_tensor(out=u32[0:32, 0:T], in0=PS_C[0:32, 0:T],
                                        in1=cosk_sb[:, 0:T], op=OP.mult)
                        v.tensor_tensor(out=bc_f[0:32, 0:T],
                                        in0=PS_C[32:64, 0:T],
                                        in1=sink_sb[:, 0:T], op=OP.mult)
                        ti(v.tensor_tensor(out=k_h[0:32, 0:T],
                                           in0=u32[0:32, 0:T],
                                           in1=bc_f[0:32, 0:T],
                                           op=OP.subtract), "vAT")
                    krv0 = cur("vAT")

                    @block.tensor
                    def _(t, krv0=krv0):
                        t.wait_ge(SEM("vAT"), krv0)
                        t.matmul(PS_C[0:RD, 0:512], wkr_bf[:, 0:RD],
                                 k_h[:, T:T + 512], start=True, stop=True)
                        ti(t.matmul(PS_C[0:RD, 512:1024], wkr_bf[:, 0:RD],
                                    k_h[:, T + 512:2 * T], start=True,
                                    stop=True), "mAT")
                    kr1 = cur("mAT")

                    @block.vector
                    def _(v, kr1=kr1):
                        v.wait_ge(SEM("mAT"), kr1)
                        v.tensor_tensor(out=u32[0:32, 0:T], in0=PS_C[0:32, 0:T],
                                        in1=cosk_sb[:, T:2 * T], op=OP.mult)
                        v.tensor_tensor(out=bc_f[0:32, 0:T],
                                        in0=PS_C[32:64, 0:T],
                                        in1=sink_sb[:, T:2 * T], op=OP.mult)
                        v.tensor_tensor(out=k_h[0:32, T:2 * T],
                                        in0=u32[0:32, 0:T], in1=bc_f[0:32, 0:T],
                                        op=OP.subtract)
                        v.memset(k_h[32:64, 0:S], 0.0)
                        ti(v.memset(k_h[32:33, 0:S], 1.0), "vAT")
                    k_rope = cur("vAT")

                    @block.scalar
                    def _(s_, k_rope=k_rope):
                        s_.wait_ge(SEM("vAT"), k_rope)
                        ti(s_.activation(out=scr2k[:, 0:S], in_=k_h[:, 0:S],
                                         func=AF.Square), "eAT")
                    ksq_s = cur("eAT")

                    @block.tensor
                    def _(t, ksq_s=ksq_s):
                        t.wait_ge(SEM("eAT"), ksq_s)
                        t.matmul(PS_C[0:1, 0:512], ones_bf[:, 0:1],
                                 scr2k[:, 0:512], start=True, stop=True)
                        t.matmul(PS_C[0:1, 512:1024], ones_bf[:, 0:1],
                                 scr2k[:, 512:1024], start=True, stop=True)
                        t.matmul(PS_D[0:1, 0:512], ones_bf[:, 0:1],
                                 scr2k[:, 1024:1536], start=True, stop=True)
                        ti(t.matmul(PS_D[0:1, 512:1024], ones_bf[:, 0:1],
                                    scr2k[:, 1536:2048], start=True,
                                    stop=True), "mAT")
                    kn_t = cur("mAT")

                    @block.vector
                    def _(v, kn_t=kn_t):
                        v.wait_ge(SEM("mAT"), kn_t)
                        v.tensor_reduce(out=tiny[0:1, 3:4],
                                        in_=PS_C[0:1, 0:T], axis=AX, op=OP.max)
                        v.tensor_reduce(out=tiny[0:1, 4:5],
                                        in_=PS_D[0:1, 0:T], axis=AX, op=OP.max)
                        vpad(v)
                        ti(v.tensor_tensor(out=tiny[0:1, 2:3],
                                           in0=tiny[0:1, 3:4],
                                           in1=tiny[0:1, 4:5], op=OP.max),
                           "vAT")
                    k2m_v = cur("vAT")

                    q_mm0 = cur("mAT")

                    @block.tensor
                    def _(t, h=h, k2m_v=k2m_v):
                        t.wait_ge(SEM("vAT"), k2m_v)
                        for qc in range(QCC):
                            t.matmul(PS_A[:, 0:512],
                                     qslb[h % 2][:, qc * 128:(qc + 1) * 128],
                                     cq[:, qc * T:qc * T + 512],
                                     start=(qc == 0), stop=(qc == QCC - 1))
                            i2 = t.matmul(
                                PS_A[:, 512:1024],
                                qslb[h % 2][:, qc * 128:(qc + 1) * 128],
                                cq[:, qc * T + 512:(qc + 1) * T],
                                start=(qc == 0), stop=(qc == QCC - 1))
                        ti(i2, "mAT")
                    q_mm = cur("mAT")
                    slc_used.append(q_mm)

                    @block.scalar
                    def _(s_, h=h, q_mm=q_mm):
                        s_.wait_ge(SEM("mAT"), q_mm)
                        ti(s_.activation(out=q_h[:], in_=PS_A[:, 0:T],
                                         func=AF.Identity, scale=ISD,
                                         bias=buq_sb[:, h:h + 1]), "eAT")
                    q_ev = cur("eAT")

                    @block.tensor
                    def _(t, q_ev=q_ev):
                        t.wait_ge(SEM("eAT"), q_ev)
                        t.matmul(PS_D[0:RD, 0:512], wqr_bf[:, 0:RD],
                                 q_h[:, 0:512], start=True, stop=True)
                        ti(t.matmul(PS_D[0:RD, 512:1024], wqr_bf[:, 0:RD],
                                    q_h[:, 512:1024], start=True, stop=True),
                           "mAT")
                    qr_t = cur("mAT")

                    @block.vector
                    def _(v, qr_t=qr_t):
                        v.wait_ge(SEM("mAT"), qr_t)
                        v.tensor_tensor(out=u32[0:32, 0:T], in0=PS_D[0:32, 0:T],
                                        in1=cosq_sb[:], op=OP.mult)
                        v.tensor_tensor(out=bc_f[0:32, 0:T],
                                        in0=PS_D[32:64, 0:T],
                                        in1=sinq_sb[:], op=OP.mult)
                        v.tensor_tensor(out=q_h[0:32, :], in0=u32[0:32, 0:T],
                                        in1=bc_f[0:32, 0:T], op=OP.subtract)
                        ti(v.memset(q_h[32:64, :], 0.0), "vAT")
                    q_rope = cur("vAT")

                    @block.scalar
                    def _(s_, q_rope=q_rope):
                        s_.wait_ge(SEM("vAT"), q_rope)
                        ti(s_.activation(out=scr2k[:, 0:T], in_=q_h[:],
                                         func=AF.Square), "eAT")
                    qsq_s = cur("eAT")

                    @block.tensor
                    def _(t, qsq_s=qsq_s):
                        t.wait_ge(SEM("eAT"), qsq_s)
                        t.matmul(PS_C[0:1, 0:512], ones_bf[:, 0:1],
                                 scr2k[:, 0:512], start=True, stop=True)
                        ti(t.matmul(PS_C[0:1, 512:1024], ones_bf[:, 0:1],
                                    scr2k[:, 512:1024], start=True, stop=True),
                           "mAT")
                    qn_t = cur("mAT")

                    @block.vector
                    def _(v, qn_t=qn_t):
                        v.wait_ge(SEM("mAT"), qn_t)
                        ti(v.tensor_scalar(out=srow16[0:1, :],
                                           in0=PS_C[0:1, 0:T],
                                           scalar1=tiny[0:1, 2:3],
                                           scalar2=-0.505,
                                           op0=OP.add, op1=OP.mult), "vAT")
                    negm_v = cur("vAT")

                    @block.gpsimd
                    def _(g, negm_v=negm_v):
                        g.wait_ge(SEM("vAT"), negm_v)
                        ti(g.dma_start(out=q_h[32:33, :], in_=srow16[0:1, :]),
                           "dRW", 16)
                    row_d = cur("dRW")

                    mST0, eEX0, mCX0 = cur("mST"), cur("eEX"), cur("mCX")

                    @block.tensor
                    def _(t, row_d=row_d, v_done=v_done,
                          mST0=mST0, eEX0=eEX0, mCX0=mCX0):
                        t.wait_ge(SEM("dRW"), row_d)
                        t.wait_ge(SEM("vAT"), v_done)
                        for skc in range(SKC):
                            pp = (PS_A, PS_D)[skc % 2]
                            if skc >= 2:
                                t.wait_ge(SEM("eEX"), eEX0 + skc - 1)
                            t.matmul(pp[:, 0:512],
                                     k_h[:, skc * 128:(skc + 1) * 128],
                                     q_h[:, 0:512], start=True, stop=True)
                            ti(t.matmul(pp[:, 512:1024],
                                        k_h[:, skc * 128:(skc + 1) * 128],
                                        q_h[:, 512:1024], start=True,
                                        stop=True), "mST")
                            if skc >= 1:
                                c = skc - 1
                                t.wait_ge(SEM("eEX"), eEX0 + c + 1)
                                pb = scr2k[:, (c % 2) * T:(c % 2 + 1) * T]
                                t.matmul(PS_C[0:1, 0:512], ones_bf[:, 0:1],
                                         pb[:, 0:512], start=(c == 0),
                                         stop=(c == SKC - 1))
                                t.matmul(PS_C[0:1, 512:1024], ones_bf[:, 0:1],
                                         pb[:, 512:1024], start=(c == 0),
                                         stop=(c == SKC - 1))
                                t.matmul(PS_B[:, 0:512],
                                         v_h[:, c * 128:(c + 1) * 128],
                                         pb[:, 0:512], start=(c == 0),
                                         stop=(c == SKC - 1))
                                ti(t.matmul(PS_B[:, 512:1024],
                                            v_h[:, c * 128:(c + 1) * 128],
                                            pb[:, 512:1024], start=(c == 0),
                                            stop=(c == SKC - 1)), "mCX")
                        c = SKC - 1
                        t.wait_ge(SEM("eEX"), eEX0 + c + 1)
                        pb = scr2k[:, (c % 2) * T:(c % 2 + 1) * T]
                        t.matmul(PS_C[0:1, 0:512], ones_bf[:, 0:1],
                                 pb[:, 0:512], start=False, stop=True)
                        t.matmul(PS_C[0:1, 512:1024], ones_bf[:, 0:1],
                                 pb[:, 512:1024], start=False, stop=True)
                        t.matmul(PS_B[:, 0:512],
                                 v_h[:, c * 128:(c + 1) * 128],
                                 pb[:, 0:512], start=False, stop=True)
                        ti(t.matmul(PS_B[:, 512:1024],
                                    v_h[:, c * 128:(c + 1) * 128],
                                    pb[:, 512:1024], start=False, stop=True),
                           "mCX")

                    @block.scalar
                    def _(s_, mST0=mST0, mCX0=mCX0):
                        for skc in range(SKC):
                            s_.wait_ge(SEM("mST"), mST0 + skc + 1)
                            if skc >= 2:
                                s_.wait_ge(SEM("mCX"), mCX0 + skc - 1)
                            pp = (PS_A, PS_D)[skc % 2]
                            ti(s_.activation(
                                out=scr2k[:, (skc % 2) * T:(skc % 2 + 1) * T],
                                in_=pp[:, 0:T], func=AF.Exp), "eEX")
                    ctx_mm = cur("mCX")

                    @block.scalar
                    def _(s_, ctx_mm=ctx_mm):
                        s_.wait_ge(SEM("mCX"), ctx_mm)
                        ti(s_.activation(out=st_row[0:1, 0:T],
                                         in_=PS_C[0:1, 0:T], func=AF.Ln),
                           "eAT")
                    ln_s = cur("eAT")

                    @block.tensor
                    def _(t, ln_s=ln_s):
                        t.wait_ge(SEM("eAT"), ln_s)
                        t.matmul(PS_C[:, 0:512], ones_f[0:1, 0:128],
                                 st_row[0:1, 0:512], start=True, stop=True)
                        ti(t.matmul(PS_C[:, 512:1024], ones_f[0:1, 0:128],
                                    st_row[0:1, 512:1024], start=True,
                                    stop=True), "mAT")
                    lnb_t = cur("mAT")

                    @block.scalar
                    def _(s_, lnb_t=lnb_t):
                        s_.wait_ge(SEM("mAT"), lnb_t)
                        ti(s_.activation(out=u32[:], in_=PS_C[:, 0:T],
                                         func=AF.Exp, scale=-1.0), "eAT")
                    prev_inv = cur("eAT")

                    @block.vector
                    def _(v, h=h, inv_s=prev_inv, ctx_mm=ctx_mm):
                        v.wait_ge(SEM("eAT"), inv_s)
                        v.wait_ge(SEM("mCX"), ctx_mm)
                        ti(v.tensor_tensor(out=ctx[:, h * T:(h + 1) * T],
                                           in0=PS_B[:, 0:T], in1=u32[:],
                                           op=OP.mult), "vAT")
                    head_done.append(cur("vAT"))

                sync_all()
                free("k_h", "v_h", "q_h", "scr2k", "kslb0", "kslb1",
                     "vslb0", "vslb1", "qslb0", "qslb1", "ckv", "cq",
                     "wqr_bf", "wkr_bf", "cosq_sb", "sinq_sb", "cosk_sb",
                     "sink_sb", "buvb_sb")

                wob = alloc("wob", [128, DC * T], BF16)
                for half in range(2):
                    @block.gpsimd
                    def _(g, half=half, tgt=cur("mAT")):
                        if half == 1:
                            g.wait_ge(SEM("mAT"), tgt)
                        ti(g.dma_start(
                            out=wob[:].rearrange("p (c o) -> p c o", o=T),
                            in_=wo[:, half * T:(half + 1) * T].rearrange(
                                "(c p) o -> p c o", p=128)), "dAW", 16)
                    wo_d = cur("dAW")
                    wo_ev0 = cur("vAT")

                    @block.tensor
                    def _(t, half=half, wo_d=wo_d, wo_ev0=wo_ev0):
                        t.wait_ge(SEM("dAW"), wo_d)
                        for dc in range(8):
                            pp = (PS_A, PS_B)[dc % 2]
                            if dc >= 2:
                                t.wait_ge(SEM("vAT"), wo_ev0 + dc - 1)
                            for hc in range(H):
                                t.matmul(pp[:, 0:512],
                                         wob[:, hc * T + dc * 128:
                                             hc * T + (dc + 1) * 128],
                                         ctx[:, hc * T:hc * T + 512],
                                         start=(hc == 0), stop=(hc == H - 1))
                                i2 = t.matmul(
                                    pp[:, 512:1024],
                                    wob[:, hc * T + dc * 128:
                                        hc * T + (dc + 1) * 128],
                                    ctx[:, hc * T + 512:(hc + 1) * T],
                                    start=(hc == 0), stop=(hc == H - 1))
                            ti(i2, "mAT")
                    wo_mm0 = cur("mAT") - 8

                    @block.vector
                    def _(v, half=half, wo_mm0=wo_mm0):
                        for dc in range(8):
                            v.wait_ge(SEM("mAT"), wo_mm0 + dc + 1)
                            col = half * 8 + dc
                            pp = (PS_A, PS_B)[dc % 2]
                            ti(v.scalar_tensor_tensor(
                                out=x_sb[:, col * T:(col + 1) * T],
                                in0=pp[:, 0:T],
                                scalar=bo_sb[:, col:col + 1],
                                in1=x_sb[:, col * T:(col + 1) * T],
                                op0=OP.add, op1=OP.add), "vAT")
                sync_all()
                free("wob")

            mpos = alloc("mpos", [128, T])
            mneg = alloc("mneg", [128, T])
            nx_v = ln_phase(act16, True, mpos, mneg)
            sync_all()
            free("x_sb")
            h_sb = alloc("h_sb", [128, OCH * T], FP16)
            tstg = [alloc(f"tstg{i}", [128, 4096], BF16) for i in range(2)]
            ffn(True, act16, h_sb, mpos, mneg, tstg, nx_v)
            free("h_sb", "mpos", "mneg", "tstg0", "tstg1", "babs", "wstg0",
                 "wstg1")

            if upto == 1:
                @block.sync
                def _(sy):
                    for c in range(DC):
                        ti(sy.dma_start(
                            out=out_fm[c * 128:(c + 1) * 128, :],
                            in_=xres_d[:, c * T:(c + 1) * T]), "dXO", 16)
                sync_all()
            if upto >= 2:
                attention()
            if upto == 2:
                @block.sync
                def _(sy):
                    for c in range(DC):
                        ti(sy.dma_start(
                            out=out_fm[c * 128:(c + 1) * 128, :],
                            in_=x_sb[:, c * T:(c + 1) * T]), "dXO", 16)
                sync_all()
            if upto >= 3:
                mpos = alloc("mpos2", [128, T])
                mneg = alloc("mneg2", [128, T])
                babs = alloc("babs2", [128, 4096], BF16)
                nx_v = ln_phase(act16, True, mpos, mneg)

                @block.sync
                def _(sy):
                    ti(sy.dma_start(out=xres_d, in_=x_sb[:]), "dXO", 16)
                sync_all()
                free("x_sb2")
                h_sb = alloc("h_sb2", [128, OCH * T], FP16)
                tstg = [alloc(f"tstg{i}b", [128, 4096], BF16)
                        for i in range(2)]
                ffn(False, act16, h_sb, mpos, mneg, tstg, nx_v)
                free("h_sb2", "mpos2", "mneg2", "babs2", "tstg0b", "tstg1b")

    return nc


def _stage_inputs(inputs):
    f32 = lambda a: np.ascontiguousarray(np.asarray(a, dtype=np.float32))
    x = f32(inputs["L0_attn_hidden_states"])
    wupT = f32(np.asarray(inputs["L0_ffn_w_up"]).T)
    wdownT = f32(np.asarray(inputs["L0_ffn_w_down"]).T)
    col = lambda v, c: f32(np.asarray(v, dtype=np.float32).reshape(c, 128).T)
    theta = 10000.0
    inv_freq = 1.0 / theta ** (np.arange(0, RD, 2, dtype=np.float32) / RD)
    tpos = np.arange(S, dtype=np.float32)
    fr = tpos[:, None] * inv_freq[None, :]
    cosk = f32(np.cos(fr).T)
    sink = f32(np.sin(fr).T)
    ident = np.eye(128, dtype=np.float32)
    buvb = f32(np.broadcast_to(np.asarray(inputs["L0_attn_b_UV"])[None, :],
                               (128, DM)))
    shared = dict(
        wdkv=f32(inputs["L0_attn_W_DKV"]), wuk=f32(inputs["L0_attn_W_UK"]),
        wuv=f32(inputs["L0_attn_W_UV"]), wdq=f32(inputs["L0_attn_W_DQ"]),
        wuq=f32(inputs["L0_attn_W_UQ"]), wqr=f32(inputs["L0_attn_W_QR"]),
        wkr=f32(inputs["L0_attn_W_KR"]), wo=f32(inputs["L0_attn_W_O"]),
        wupT=wupT, wdownT=wdownT,
        b_dkv=col(inputs["L0_attn_b_DKV"], KVCC),
        b_uk=col(inputs["L0_attn_b_UK"], DC),
        b_dq=col(inputs["L0_attn_b_DQ"], QCC),
        b_uq=col(inputs["L0_attn_b_UQ"], DC),
        b_o=col(inputs["L0_attn_b_O"], DC), buv_b=buvb,
        cosk=cosk, sink=sink, ident_in=ident,
    )
    in_maps = []
    for c in range(8):
        seq, half = c // 2, c % 2
        xs = f32(x[seq, half * T:(half + 1) * T, :].T)
        m = dict(shared)
        m["x_fm"] = xs
        m["cosq"] = f32(cosk[:, half * T:(half + 1) * T])
        m["sinq"] = f32(sink[:, half * T:(half + 1) * T])
        in_maps.append(m)
    return in_maps


_CACHE = {}


def kernel(upto=3, **inputs):
    if upto not in _CACHE:
        _CACHE[upto] = build_nc(upto)
    nc = _CACHE[upto]
    in_maps = _stage_inputs(inputs)
    res = run_bass_kernel_spmd(nc, in_maps, core_ids=list(range(8)))
    out = np.empty((B, S, DM), dtype=np.float32)
    for c in range(8):
        seq, half = c // 2, c % 2
        out[seq, half * T:(half + 1) * T, :] = res.results[c]["out_fm"].T
    return out
